# version 30
# baseline (speedup 1.0000x reference)
import math
import numpy as np
import concourse.bass as bass
import concourse.mybir as mybir
from concourse.bass_utils import run_bass_kernel_spmd
from concourse.bass_core_rust import IndirectOffsetOnAxis

F32 = mybir.dt.float32
BF16 = mybir.dt.bfloat16
I32 = mybir.dt.int32
AF = mybir.ActivationFunctionType
OP = mybir.AluOpType
AX = mybir.AxisListType

LN_EPS = 1e-5
WKV_EPS = 1e-6
EXP_CLAMP = float(math.exp(30.0))
BIG = 1.0e9

D = 1024
NCT = D // 128


def make_cfg(T=4096, DF=4096, CAP=1280, E=8, chunk=256, debug=False, n_cores=8):
    assert T % chunk == 0 and chunk % 128 == 0
    assert CAP % 128 == 0
    return dict(T=T, DF=DF, CAP=CAP, E=E, chunk=chunk, debug=debug,
                n_cores=n_cores, nchunk=T // chunk, tt=chunk // 128,
                nft=DF // 128)


class Prog:

    ENGINES = ("pe", "dve", "act", "pool", "sp")
    AUTO = ("dve", "act", "pool")
    MAX_OUTSTANDING = 24
    DMA_SLOTS = 8
    DMA_SLOTS_BY_ENG = {"pool": 1}

    def __init__(self):
        self.streams = {e: [] for e in self.ENGINES}
        self.cnt = {}
        self.sems = {}
        self.last_wait = {}

    def do(self, eng, fn, inc=None, inc_by=1, nochain=False):
        if eng == "pe" and inc is None:
            inc = "e_pe"
        if eng in self.AUTO:
            s = "e_" + eng
            prev = self.cnt.get(s, 0)
            if prev > 0 and not nochain:
                self.last_wait[(eng, s)] = max(self.last_wait.get((eng, s), 0), prev)
                self.streams[eng].append(
                    lambda e, s=s, v=prev: e.wait_ge(self.sems[s], v))
            if inc is None or inc == s:
                self.cnt[s] = prev + 1
                tok = (s, prev + 1)

                def wrapped(e, fn=fn, s=s):
                    fn(e).then_inc(self.sems[s], 1)
                self.streams[eng].append(wrapped)
                return tok
        tok = None
        if inc is not None:
            self.cnt.setdefault(inc, 0)
            nxt = self.cnt[inc] + inc_by
            lw = self.last_wait.get((eng, inc), 0)
            if nxt - lw > self.MAX_OUTSTANDING and inc_by == 1:
                v = nxt - self.MAX_OUTSTANDING // 2
                self.streams[eng].append(
                    lambda e, s=inc, v=v: e.wait_ge(self.sems[s], v))
                self.last_wait[(eng, inc)] = v
            self.cnt[inc] = nxt
            tok = (inc, nxt)

            def wrapped(e, fn=fn, s=inc, k=inc_by):
                fn(e).then_inc(self.sems[s], k)
            self.streams[eng].append(wrapped)
        else:
            self.streams[eng].append(fn)
        return tok

    def _dma_slot(self, eng):
        nq = "n_" + eng
        n = self.cnt.get(nq, 0)
        self.cnt[nq] = n + 1
        nslots = self.DMA_SLOTS_BY_ENG.get(eng, self.DMA_SLOTS)
        q = f"q_{eng}{n % nslots}"
        uses = self.cnt.get(q, 0)
        if uses > 0:
            self.last_wait[(eng, q)] = max(self.last_wait.get((eng, q), 0), uses)
            self.streams[eng].append(
                lambda e, s=q, v=uses: e.wait_ge(self.sems[s], v))
        return q

    def dma(self, eng, out, in_):
        q = self._dma_slot(eng)
        return self.do(eng, lambda e, o=out, i=in_: e.dma_start(out=o, in_=i),
                       inc=q, inc_by=16)

    def idma(self, eng, fn):
        q = self._dma_slot(eng)
        return self.do(eng, fn, inc=q, inc_by=16)

    def dma_fence(self, eng):
        toks = []
        for s in range(self.DMA_SLOTS):
            q = f"q_{eng}{s}"
            if self.cnt.get(q, 0):
                toks.append((q, self.cnt[q]))
        return toks

    def fence(self, eng):
        toks = list(self.dma_fence(eng))
        s = "e_" + eng
        if self.cnt.get(s, 0):
            toks.append((s, self.cnt[s]))
        return toks

    def fence_all(self):
        return [t for e in self.ENGINES for t in self.fence(e)]

    def wait(self, eng, tok):
        if tok is None:
            return
        s, v = tok
        if self.last_wait.get((eng, s), -1) >= v:
            return
        self.last_wait[(eng, s)] = v
        self.streams[eng].append(lambda e, s=s, v=v: e.wait_ge(self.sems[s], v))

    def wait_all(self, eng, toks):
        best = {}
        for t in toks:
            if t is None:
                continue
            s, v = t
            best[s] = max(best.get(s, 0), v)
        for s, v in sorted(best.items()):
            self.wait(eng, (s, v))


def build_nc(cfg):
    T, DF, CAP, E = cfg["T"], cfg["DF"], cfg["CAP"], cfg["E"]
    CH, NCH, TTC, NFT = cfg["chunk"], cfg["nchunk"], cfg["tt"], cfg["nft"]
    DBG = cfg["debug"]
    n_cores = cfg["n_cores"]
    NTT = T // 128
    inv_d = 1.0 / D

    nc = bass.Bass("TRN2", target_bir_lowering=False)
    P = Prog()

    EI = dict(kind="ExternalInput")
    dx = nc.dram_tensor("x", [T, D], F32, **EI)
    dWkT = nc.dram_tensor("WkT", [D, D], F32, **EI)
    dWvT = nc.dram_tensor("WvT", [D, D], F32, **EI)
    dWrT = nc.dram_tensor("WrT", [D, D], F32, **EI)
    dWoT = nc.dram_tensor("WoT", [D, D], F32, **EI)
    dWffkT = nc.dram_tensor("WffkT", [D, DF], F32, **EI)
    dWffvT = nc.dram_tensor("WffvT", [DF, D], F32, **EI)
    dtd = nc.dram_tensor("td", [D], F32, **EI)
    dtf = nc.dram_tensor("tf", [D], F32, **EI)
    dg1 = nc.dram_tensor("ln1g", [D], F32, **EI)
    db1 = nc.dram_tensor("ln1b", [D], F32, **EI)
    dg2 = nc.dram_tensor("ln2g", [D], F32, **EI)
    db2 = nc.dram_tensor("ln2b", [D], F32, **EI)
    drw = nc.dram_tensor("router_w", [D, E], F32, **EI)
    drb = nc.dram_tensor("router_b", [E], F32, **EI)
    dw1 = nc.dram_tensor("w1e", [D, DF], F32, **EI)
    db1e = nc.dram_tensor("b1e", [DF], F32, **EI)
    dw2 = nc.dram_tensor("w2e", [DF, D], F32, **EI)
    db2e = nc.dram_tensor("b2e", [D], F32, **EI)
    dow1 = nc.dram_tensor("ow1", [D, DF], F32, **EI)
    dob1 = nc.dram_tensor("ob1", [DF], F32, **EI)
    dow2 = nc.dram_tensor("ow2", [DF, D], F32, **EI)
    dob2 = nc.dram_tensor("ob2", [D], F32, **EI)
    dpfx = nc.dram_tensor("prefix_mask", [8], F32, **EI)
    dout = nc.dram_tensor("out", [T, D], F32, kind="ExternalOutput")

    dts = nc.dram_tensor("tscratch", [cfg["nchunk"], 128, (D // 128) * cfg["chunk"]], F32)
    dX2 = nc.dram_tensor("X2scratch", [T, D], F32)
    dt2tm = nc.dram_tensor("t2tm", [T, D], BF16)
    dt2cm = nc.dram_tensor("t2cm", [cfg["nchunk"], 128, (D // 128) * cfg["chunk"]], BF16)
    deid = nc.dram_tensor("eidscr", [T], F32)
    dstat = nc.dram_tensor("statscr", [2, T], F32)
    dcum = nc.dram_tensor("cumscr", [E, T], F32)
    ddf = nc.dram_tensor("destf", [T], F32)
    divm = nc.dram_tensor("ivmscr", [T], F32)
    dslot = nc.dram_tensor("slotbuf", [E * CAP, D], BF16)
    dmybuf = nc.dram_tensor("mybuf", [CAP, D], BF16)
    dytm = nc.dram_tensor("ytm", [CAP, D], F32)
    dyall = nc.dram_tensor("yall", [E * CAP, D], F32)
    dcnt = nc.dram_tensor("cnt", [E], F32)
    dzero = nc.dram_tensor("dzero", [128 * cfg["chunk"] * min(
        cfg["DF"] // 128, max(1, 1024 // cfg["chunk"]))], BF16)
    dow1b = nc.dram_tensor("ow1b", [D, DF], BF16)
    dow2b = nc.dram_tensor("ow2b", [DF, D], BF16)
    dw1b = nc.dram_tensor("w1b", [D, DF], BF16)
    dw2b = nc.dram_tensor("w2b", [DF, D], BF16)
    dcall = nc.dram_tensor("cntall", [n_cores * E], F32)

    dbg = {}
    if DBG:
        for nm, shp in [("dbg_t", [D, T]), ("dbg_xl", [D, T]),
                        ("dbg_tfin", [D, T]), ("dbg_t2", [T, D]),
                        ("dbg_eid", [T]), ("dbg_dest", [T]), ("dbg_rank", [T]),
                        ("dbg_buf", [CAP, D]), ("dbg_y", [CAP, D]),
                        ("dbg_cnt", [E]), ("dbg_call", [n_cores * E]),
                        ("dbg_base", [8, 2]), ("dbg_pfx", [8, 1]), ("dbg_pbc", [8, 16]), ("dbg_brow", [8, 16]), ("dbg_ivm", [T]), ("dbg_t2cm", [D, T]), ("dbg_uov", [DF, CH]), ("dbg_z", [D, T]), ("dbg_csb", [8, E])]:
            dbg[nm] = nc.dram_tensor(nm, shp, F32, kind="ExternalOutput")

    from contextlib import ExitStack
    stack = ExitStack()

    def sb(name, shape, dtype=F32):
        return stack.enter_context(nc.sbuf_tensor(name, shape, dtype))

    def ps(name, shape, dtype=F32):
        return stack.enter_context(nc.psum_tensor(name, shape, dtype))

    ident_f = sb("ident_f", [128, 128])
    ident_b = sb("ident_b", [128, 128], BF16)
    cst = sb("cst", [128, 10])
    CVALS = [LN_EPS, WKV_EPS, -0.5, -1.0, 1.0, 0.0, float(CAP), BIG, 0.5,
             float(math.log(0.5))]
    LN_HALF = float(math.log(0.5))

    def C(v):
        return cst[:, CVALS.index(v):CVALS.index(v) + 1]

    def C1(v):
        return cst[0:1, CVALS.index(v):CVALS.index(v) + 1]

    ones_row = sb("ones_row", [1, 128])
    ones_col = sb("ones_col", [128, 1])
    cdecay = sb("cdecay", [128, NCT])
    ctf = sb("ctf", [128, NCT])
    cg1 = sb("cg1", [128, NCT])
    cb1 = sb("cb1", [128, NCT])
    cg2 = sb("cg2", [128, NCT])
    cb2 = sb("cb2", [128, NCT])
    cob2 = sb("cob2", [128, NCT])
    cb2e = sb("cb2e", [128, NCT])
    iota8 = sb("iota8", [128, 1])
    cb1e = sb("cb1e", [128, NFT])
    cob1 = sb("cob1", [128, NFT])
    rb_b = sb("rb_b", [128, E])
    pfx_sb = sb("pfx_sb", [8, 1])
    call_sb = sb("call_sb", [8, E])
    base8 = sb("base8", [128, 2])
    raw_cols = sb("raw_cols", [128, 2 * NCT])
    carry_n = sb("carry_n", [128, NCT])
    carry_d = sb("carry_d", [128, NCT])
    carry8 = sb("carry8", [8, 1])
    stat = sb("stat", [128, 16])
    eid_tiles = sb("eid_tiles", [128, NTT])
    desti_t = sb("desti_t", [128, NTT], I32)
    idx_u = sb("idx_u", [128, 8], mybir.dt.uint32)


    AW1 = sb("AW1", [128, NCT, max(DF, 4 * D)], BF16)
    AW2 = sb("AW2", [128, NFT, D], BF16)
    rw_sb = sb("rw_sb", [128, NCT, E])
    A_xtm = [sb(f"A_xtm{i}", [128, TTC * D]) for i in range(2)]
    A_t0 = sb("A_t0", [128, NCT, CH])
    A_xlf_base = sb("A_xlf", [128, NCT * CH])
    A_xlf = A_xlf_base.rearrange("p (a b) -> p a b", a=NCT)
    A_xlb = sb("A_xlb", [128, NCT, CH], BF16)
    A_rwb = sb("A_rwb", [128, NCT, CH], BF16)
    A_dump = sb("A_dump", [128, 2048])

    scr_rows = A_dump[0:1, 2048 - 2 * CH:2048]
    rows = A_dump[0:1, 128:128 + 3 * CH + 32]
    A_u2 = sb("A_u2", [128, NFT, CH], BF16)
    A_ot3 = sb("A_ot3", [128, 12, CH])
    A_otr = [A_ot3[:, i, :] for i in range(12)]
    oh_sb = A_otr[0]
    cum_sb = A_otr[1]
    bcol_kvr = sb("bcol_kvr", [128, 3 * 16])
    bcol2 = sb("bcol2", [128, 2 * NFT])
    repb = sb("repb", [128, NCT, 16], BF16)

    xlb_flat = A_xlb[:, :, :].rearrange("p a b -> p (a b)")
    u2_flat = A_u2[:, :, :].rearrange("p a b -> p (a b)")
    rwb_flat = A_rwb[:, :, :].rearrange("p a b -> p (a b)")
    xlf_flat = A_xlf_base

    pT = [ps(f"pT{i}", [128, 512]) for i in range(2)]
    pB = [ps(f"pB{i}", [128, 512]) for i in range(2)]
    pM = [ps(f"pM{i}", [128, 512]) for i in range(4)]

    for s in ["e_pe", "e_dve", "e_act", "e_pool", "c_cnt", "c_rs", "c_yag"]:
        P.cnt.setdefault(s, 0)

    bank_free = {i: None for i in range(4)}
    rbnds = [None]

    def _setup_rbnds(e):
        rbnds[0] = e.alloc_register("bnds")
        return e.reg_mov(rbnds[0], E * CAP - 1)
    pB_free = [None, None]
    pT_free = [None, None]

    P.do("pool", _setup_rbnds)
    P.do("pool", lambda e: e.iota(ident_f[:, :], pattern=[[1, 128]], base=0,
                                  channel_multiplier=-1,
                                  allow_small_or_imprecise_dtypes=True))
    tok_id0 = P.do("pool", lambda e: e.tensor_single_scalar(
        ident_f[:, :], ident_f[:, :], 0.0, OP.is_equal))
    P.wait("dve", tok_id0)
    tok_idb = P.do("dve", lambda e: e.tensor_copy(ident_b[:, :], ident_f[:, :]))
    P.do("pool", lambda e: e.iota(iota8[:, :], pattern=[[1, 1]], base=0,
                                  channel_multiplier=1,
                                  allow_small_or_imprecise_dtypes=True))
    for _i, _v in enumerate(CVALS):
        P.do("pool", lambda e, i=_i, v=_v: e.memset(cst[:, i:i + 1], v))
    P.do("pool", lambda e: e.memset(ones_row[:, :], 1.0))
    P.do("pool", lambda e: e.memset(ones_col[:, :], 1.0))
    P.do("pool", lambda e: e.memset(carry_n[:, :], 0.0))
    P.do("pool", lambda e: e.memset(carry_d[:, :], 0.0))
    tok_const = P.do("pool", lambda e: e.memset(carry8[:, :], 0.0))

    tw = []
    for j, dsrc in enumerate([dWkT, dWvT, dWrT, dWoT]):
        tw.append(P.dma("pool", AW1[:, :, j * D:(j + 1) * D],
                        dsrc.ap().rearrange("(c p) f -> p c f", p=128)))
    tok_ffv = P.dma("pool", AW2[:, :, :],
                    dWffvT.ap().rearrange("(h p) f -> p h f", p=128))
    tok_cv = [P.dma("pool", dst.ap(), srcd.ap())
              for dst, srcd in [(dow1b, dow1), (dow2b, dow2)]]
    nzt = min(NFT, max(1, D // CH))
    zcols = nzt * CH
    tok_z0 = P.do("pool", lambda e: e.memset(A_u2[:, 0:nzt, :], 0.0))
    zflat = A_u2[:, 0:nzt, :].rearrange("p a b -> p (a b)")
    slot_flat = dslot.ap().rearrange("a b -> (a b)")
    total_el = E * CAP * D
    per_dma = 128 * zcols
    P.wait("pool", tok_z0)
    tok_dz0 = P.dma("pool", dzero.ap().rearrange("(p q) -> p q", q=zcols),
                    zflat[0:128, 0:zcols])
    P.wait("act", tok_dz0)
    P.wait("sp", tok_dz0)
    n_zdma = (total_el + per_dma - 1) // per_dma
    zfill_done = []
    zfill_state = [0]

    def emit_zero_copies(k):
        if cfg.get("skip_zero"):
            return
        i0 = zfill_state[0]
        for i in range(i0, min(i0 + k, n_zdma)):
            e0 = i * per_dma
            nel = min(total_el - e0, per_dma)
            zfill_done.append(P.dma("act" if i % 2 else "sp",
                                    slot_flat[e0:e0 + nel], dzero.ap()[0:nel]))
        zfill_state[0] = min(i0 + k, n_zdma)
    tok_rwl = P.dma("sp", rw_sb[:, :, :],
                    drw.ap().rearrange("(c p) e -> p c e", p=128))
    tok_pfx = P.dma("sp", pfx_sb[:, :], dpfx.ap().rearrange("(a b) -> a b", b=1))

    vecsD = [(dtd, raw_cols[:, 0:NCT], NCT), (dtf, raw_cols[:, NCT:2 * NCT], NCT),
             (dg1, cg1[:, :], NCT), (db1, cb1[:, :], NCT), (dg2, cg2[:, :], NCT),
             (db2, cb2[:, :], NCT), (dob2, cob2[:, :], NCT), (db2e, cb2e[:, :], NCT),
             (db1e, cb1e[:, :], NFT), (dob1, cob1[:, :], NFT)]
    def vband(i, nn, w=128):
        r0 = (i // 8) * 32
        c0 = (i % 8) * 128
        return A_dump[r0:r0 + nn, c0:c0 + w]

    toks_vr = []
    vplaces = []
    i_rb = 0
    toks_vr.append(P.dma("sp", vband(i_rb, 1, E),
                         drb.ap().rearrange("(a b) -> a b", a=1)))
    for i0, (dv, dst, nn) in enumerate(vecsD):
        i = i0 + 1
        toks_vr.append(P.dma("sp", vband(i, nn),
                             dv.ap().rearrange("(c p) -> c p", p=128)))
        vplaces.append((i, nn, dst))

    P.wait_all("pe", toks_vr + [tok_id0, tok_const])
    pcol = 0
    tcols = []
    for (i, nn, dst) in vplaces:
        r0 = (i // 8) * 32
        P.do("pe", lambda e, i=i, nn=nn, pcol=pcol, r0=r0: e.transpose(
            pB[0][0:128, pcol:pcol + nn], vband(i, nn),
            ident_f[r0:r0 + nn, r0:r0 + nn]))
        tcols.append((pcol, nn, dst))
        pcol += nn
    tok_rbb = P.do("pe", lambda e: e.matmul(
        pB[1][:, 0:E], ones_row[0:1, :], vband(i_rb, 1, E),
        start=True, stop=True))
    P.wait("dve", ("e_pe", P.cnt["e_pe"]))
    for (pcol, nn, dst) in tcols:
        P.do("dve", lambda e, pcol=pcol, nn=nn, d=dst: e.tensor_copy(
            d, pB[0][:, pcol:pcol + nn]))
    tok_cc = P.do("dve", lambda e: e.tensor_copy(rb_b[:, :], pB[1][:, 0:E]))
    pB_free = [tok_cc, tok_cc]

    P.wait("act", tok_cc)
    P.wait("act", tok_const)
    P.do("act", lambda e: e.activation(cdecay[:, :], raw_cols[:, 0:NCT], AF.Exp))
    P.do("act", lambda e: e.activation(cdecay[:, :], cdecay[:, :], AF.Exp,
                                       scale=C(-1.0)))
    tok_dec = P.do("act", lambda e: e.activation(ctf[:, :], raw_cols[:, NCT:2 * NCT],
                                                 AF.Exp))
    P.wait("dve", tok_dec)
    P.wait("dve", tok_const)

    f_const_light = ("e_pe", P.cnt["e_pe"])
    P.wait_all("pe", tw[:3] + [tok_idb, tok_const])

    tok_repb = None
    for ct in range(NCT):
        tok_repb = P.do("dve", lambda e, ct=ct: e.tensor_scalar_mul(
            repb[:, ct, :], C(1.0).broadcast_to([128, 16]), cb1[:, ct:ct + 1]))
    P.wait("pe", tok_repb)
    for p_ in range(3):
        tok_bm = None
        for o in range(NCT):
            for ct in range(NCT):
                tok_bm = P.do("pe", lambda e, p_=p_, o=o, ct=ct: e.matmul(
                    pM[p_][:, o * 16:(o + 1) * 16],
                    AW1[:, ct, p_ * D + o * 128:p_ * D + (o + 1) * 128],
                    repb[:, ct, :], start=(ct == 0), stop=(ct == NCT - 1)))
        P.wait("dve", tok_bm)
        bview = bcol_kvr[:, p_ * 16:(p_ + 1) * 16].rearrange("p (a b) -> p a b", b=2)
        pview = pM[p_][:, 0:128].rearrange("p (a b) -> p a b", b=16)[:, :, 0:2]
        if p_ == 2:
            tok_e = P.do("dve", lambda e, bview=bview, pview=pview:
                         e.tensor_scalar_mul(bview, pview, -1.0))
        else:
            tok_e = P.do("dve", lambda e, bview=bview, pview=pview:
                         e.tensor_copy(bview, pview))
        bank_free[p_] = tok_e
    f_bias_pe = ("e_pe", P.cnt["e_pe"])
    P.wait("dve", f_bias_pe)
    tok_wfold = None
    for ct in range(NCT):
        tok_wfold = P.do("dve", lambda e, ct=ct: e.tensor_scalar_mul(
            AW1[:, ct, 0:3 * D], AW1[:, ct, 0:3 * D], cg1[:, ct:ct + 1]))
    wfold = [tok_wfold]

    f_const_pe = ("e_pe", P.cnt["e_pe"])


    def ln_stats_tm(xt, with_affine, pre_tok=None, dump=None):
        if dump is None:
            dump = A_dump
        if pre_tok is not None:
            P.wait("dve", pre_tok)
            P.wait("act", pre_tok)
        tok_sq = None
        for tt in range(TTC):
            P.do("dve", lambda e, tt=tt: e.reduce_sum(
                stat[:, tt:tt + 1], xt[:, tt * D:(tt + 1) * D], axis=AX.X),
                nochain=(tt > 0))
            tok_sq = P.do("act", lambda e, tt=tt, dump=dump: e.activation(
                dump[:, tt * D:(tt + 1) * D], xt[:, tt * D:(tt + 1) * D],
                AF.Square, accum_out=stat[:, 4 + tt:5 + tt]), nochain=(tt > 0))
        P.wait("dve", tok_sq)
        P.do("dve", lambda e: e.tensor_scalar_mul(stat[:, 8:8 + TTC],
                                                  stat[:, 0:TTC], inv_d))
        P.do("dve", lambda e: e.tensor_tensor(stat[:, 12:12 + TTC],
                                              stat[:, 8:8 + TTC],
                                              stat[:, 8:8 + TTC], OP.mult))
        tok_var = P.do("dve", lambda e: e.scalar_tensor_tensor(
            stat[:, 4:4 + TTC], stat[:, 4:4 + TTC], inv_d, stat[:, 12:12 + TTC],
            OP.mult, OP.subtract))
        P.wait("act", tok_var)
        P.do("act", lambda e: e.activation(stat[:, 12:12 + TTC], stat[:, 4:4 + TTC],
                                           AF.Ln, bias=C(LN_EPS)))
        tok_s0 = P.do("act", lambda e: e.activation(
            stat[:, 12:12 + TTC], stat[:, 12:12 + TTC], AF.Exp, scale=C(-0.5)))
        if not with_affine:
            return tok_s0
        P.wait("dve", tok_s0)
        P.do("dve", lambda e: e.tensor_tensor(stat[:, 0:TTC], stat[:, 12:12 + TTC],
                                              stat[:, 12:12 + TTC], OP.mult))
        tok_v1 = P.do("dve", lambda e: e.tensor_tensor(
            stat[:, 0:TTC], stat[:, 0:TTC], stat[:, 4:4 + TTC], OP.mult))
        P.wait("act", tok_v1)
        P.do("act", lambda e: e.activation(stat[:, 0:TTC], stat[:, 0:TTC],
                                           AF.Ln, bias=C(LN_EPS)))
        tok_s1 = P.do("act", lambda e: e.activation(stat[:, 0:TTC], stat[:, 0:TTC],
                                                    AF.Exp, scale=C(-0.5)))
        return tok_s1

    def bcast_cols(cols, tok_ready):
        P.wait("pe", tok_ready)
        P.wait_all("pe", pB_free)
        tok_tpr = None
        for (col, bi) in cols:
            for tt in range(TTC):
                tok_tpr = P.do("pe", lambda e, col=col, bi=bi, tt=tt: e.transpose(
                    pB[bi][0:1, tt * 128:(tt + 1) * 128],
                    stat[:, col + tt:col + tt + 1], ident_f[:, :]))
        P.wait("act", tok_tpr)
        tok_rows = None
        for i, (col, bi) in enumerate(cols):
            tok_rows = P.do("act", lambda e, bi=bi, i=i: e.copy(
                scr_rows[0:1, i * CH:(i + 1) * CH], pB[bi][0:1, 0:CH]))
        P.wait("pe", tok_rows)
        tok_b = []
        for i, (col, bi) in enumerate(cols):
            tok_b.append(P.do("pe", lambda e, bi=bi, i=i: e.matmul(
                pB[bi][:, 0:CH], ones_row[0:1, :],
                scr_rows[0:1, i * CH:(i + 1) * CH], start=True, stop=True)))
        return tok_b

    def mm_group(bank_idx, lhsT_fn, rhs_fn, K, consume, N=None):
        N = CH if N is None else N
        bk = pM[bank_idx]
        P.wait("pe", bank_free[bank_idx])
        tok_mm = None
        for k in range(K):
            tok_mm = P.do("pe", lambda e, k=k, bk=bk, N=N: e.matmul(
                bk[:, 0:N], lhsT_fn(k), rhs_fn(k),
                start=(k == 0), stop=(k == K - 1)))
        tok_e = consume(bk, tok_mm)
        bank_free[bank_idx] = tok_e
        return tok_e

    def cm_to_tm(src3, dst_flat, tok_src, evac_eng="act"):
        P.wait("pe", tok_src)
        tok_tm = None
        for tt in range(TTC):
            for half in range(2):
                bi = half
                P.wait("pe", pT_free[bi])
                tok_tp = None
                for j in range(4):
                    ct = half * 4 + j
                    tok_tp = P.do("pe", lambda e, ct=ct, tt=tt, bi=bi, j=j: e.transpose(
                        pT[bi][0:128, j * 128:(j + 1) * 128],
                        src3[:, ct, tt * 128:(tt + 1) * 128], ident_f[:, :]))
                P.wait(evac_eng, tok_tp)
                tok_tm = P.do(evac_eng, lambda e, tt=tt, half=half, bi=bi: e.copy(
                    dst_flat[:, tt * D + half * 512: tt * D + (half + 1) * 512],
                    pT[bi][0:128, 0:512]), nochain=(tt + half > 0))
                pT_free[bi] = tok_tm
        return tok_tm

    chunk_f = [[], []]
    otr_free = {}

    p1_prev = {}

    def p1_front_a(c):
        pr = c % 2
        xt = A_xtm[pr]
        if c >= 2:
            P.wait_all("sp", chunk_f[(c - 2) % 2])
        tok_x = P.dma("sp", xt.rearrange("p (tt f) -> p tt f", tt=TTC),
                      dx.ap()[c * CH:(c + 1) * CH, :]
                      .rearrange("(tt p) f -> p tt f", p=128))
        if c >= 1:
            P.wait("dve", p1_prev["pe_after_b"])
            P.wait("act", p1_prev["pe_after_b"])
            P.wait("act", p1_prev.get("ln2_qs_pe"))
        else:
            P.wait("act", f_const_light)
            P.wait("dve", f_const_light)
        tok_s1 = ln_stats_tm(xt, True, pre_tok=tok_x, dump=u2_flat)
        tok_u0 = None
        for tt in range(TTC):
            tok_u0 = P.do("dve", lambda e, tt=tt: e.tensor_scalar_sub(
                xt[:, tt * D:(tt + 1) * D], xt[:, tt * D:(tt + 1) * D],
                stat[:, 8 + tt:9 + tt]), nochain=(tt > 0))
        p1_prev["s1"] = tok_s1
        p1_prev["u0"] = tok_u0

    t0bufs = [A_t0, A_xlf]

    def p1_front_b(c):
        pr = c % 2
        xt = A_xtm[pr]
        t0b = t0bufs[pr]
        P.wait_all("act", p1_prev.get("srows_st") or [])
        tok_b = bcast_cols([(12, 0), (0, 1)], p1_prev["s1"])
        p1_prev["pe_after_b"] = ("e_pe", P.cnt["e_pe"])
        P.wait("pe", p1_prev["u0"])
        P.wait_all("act", p1_prev.get(("t0f", pr)) or [])
        tbanks = [pT[0], pT[1], pM[2], pM[3]]
        tfree = [pT_free[0], pT_free[1], bank_free[2], bank_free[3]]
        P.wait_all("dve", (p1_prev.get(("t0f", pr)) or [])
                   + [tok_b[0], tok_b[1], p1_prev.get("pe_after_kvr")])
        s0bc = A_dump[:, 0:CH]
        P.do("dve", lambda e, s0bc=s0bc: e.tensor_copy(s0bc, pB[0][:, 0:CH]))
        for ct in range(NCT):
            bi = ct % 4
            bk = tbanks[bi]
            P.wait("pe", tfree[bi])
            tok_tr = None
            for tt in range(TTC):
                tok_tr = P.do("pe", lambda e, ct=ct, tt=tt, bk=bk: e.transpose(
                    bk[0:128, tt * 128:(tt + 1) * 128],
                    xt[:, tt * D + ct * 128: tt * D + (ct + 1) * 128],
                    ident_f[:, :]))
            P.wait("dve", tok_tr)
            tok_ev = P.do("dve", lambda e, ct=ct, bk=bk, t0b=t0b, s0bc=s0bc:
                          e.tensor_tensor(
                              t0b[:, ct, :], bk[0:128, 0:CH], s0bc, OP.mult),
                          nochain=(ct > 0))
            tfree[bi] = tok_ev
        pT_free[0], pT_free[1] = tfree[0], tfree[1]
        bank_free[2], bank_free[3] = tfree[2], tfree[3]
        rep1 = pB[1][:, 0:CH].rearrange("p (a b) -> p a b", a=1) \
            .broadcast_to([128, NCT, CH])
        tok_xl = P.do("dve", lambda e, t0b=t0b, rep1=rep1: e.tensor_tensor(
            A_xlb[:, :, :], t0b[:, :, :], rep1, OP.mult))
        pB_free[0] = tok_xl
        pB_free[1] = tok_xl
        p1_prev["xl"] = tok_xl
        if DBG:
            P.wait("pool", tok_xl)
            tok_d = P.dma("pool", dbg["dbg_xl"].ap()
                          .rearrange("(ct p) t -> p ct t", p=128)
                          [:, :, c * CH:(c + 1) * CH],
                          A_xlb[:, :, :])
            P.wait("dve", tok_d)

    def p1_back(c):
        P.wait("pe", p1_prev["xl"])
        if c == 0:
            P.wait("pe", wfold[0])
        GO = 4
        weG = A_ot3[:, 0:GO, :]
        vvG = A_ot3[:, GO:2 * GO, :]
        sgG = A_ot3[:, 2 * GO:3 * GO, :]
        tok_rw_last = None
        for og in range(0, NCT, GO):
            os_ = list(range(og, og + GO))
            weS = {o: A_otr[o % GO] for o in os_}
            vvS = {o: A_otr[4 + o % GO] for o in os_}
            sgS = {o: A_otr[8 + o % GO] for o in os_}
            war = otr_free.get("p1")
            toks_e = {}
            for o in os_:
                def ev_k(bk, tok_mm, war=war, we=weS[o], o=o):
                    P.wait("act", tok_mm)
                    P.wait("act", war)
                    return P.do("act", lambda e, bk=bk, we=we, o=o: e.activation(
                        we[:, :], bk[:, 0:CH], AF.Exp,
                        bias=bcol_kvr[:, 2 * o:2 * o + 1]), nochain=(o % 4 > 0))
                toks_e[(o, 0)] = mm_group(o % 4,
                                          lambda k, o=o: AW1[:, k, o * 128:(o + 1) * 128],
                                          lambda k: A_xlb[:, k, :], NCT, ev_k)
            if og == 0 and c + 1 < NCH:
                p1_front_a(c + 1)
            for o in os_:
                def ev_v(bk, tok_mm, vv=vvS[o], o=o):
                    P.wait("act", tok_mm)
                    return P.do("act", lambda e, bk=bk, vv=vv, o=o: e.activation(
                        vv[:, :], bk[:, 0:CH], AF.Identity,
                        bias=bcol_kvr[:, 16 + 2 * o:16 + 2 * o + 1]),
                        nochain=True)
                toks_e[(o, 1)] = mm_group(o % 4,
                                          lambda k, o=o: AW1[:, k, D + o * 128:D + (o + 1) * 128],
                                          lambda k: A_xlb[:, k, :], NCT, ev_v)
            for o in os_:
                def ev_r(bk, tok_mm, sg=sgS[o], o=o):
                    P.wait("act", tok_mm)
                    return P.do("act", lambda e, bk=bk, sg=sg, o=o: e.activation(
                        sg[:, :], bk[:, 0:CH], AF.Exp, scale=C(-1.0),
                        bias=bcol_kvr[:, 32 + 2 * o:32 + 2 * o + 1]),
                        nochain=True)
                toks_e[(o, 2)] = mm_group(o % 4,
                                          lambda k, o=o: AW1[:, k, 2 * D + o * 128:2 * D + (o + 1) * 128],
                                          lambda k: A_xlb[:, k, :], NCT, ev_r)
            P.wait_all("dve", [toks_e[(o, 0)] for o in os_]
                       + [toks_e[(o, 1)] for o in os_]
                       + [toks_e[(o, 2)] for o in os_])
            P.do("dve", lambda e: e.scalar_tensor_tensor(
                vvG, weG, EXP_CLAMP, vvG, OP.min, OP.mult))
            P.do("dve", lambda e: e.tensor_scalar_min(weG, weG, EXP_CLAMP))
            P.do("dve", lambda e: e.tensor_scalar_add(sgG, sgG, 1.0),
                 nochain=True)
            if c == 0:
                ctfv = ctf[:, og:og + GO].rearrange("p (a b) -> p a b", b=1)
                P.do("dve", lambda e, ctfv=ctfv: e.tensor_tensor(
                    vvG[:, :, 0:1], vvG[:, :, 0:1], ctfv, OP.mult))
                P.do("dve", lambda e, ctfv=ctfv: e.tensor_tensor(
                    weG[:, :, 0:1], weG[:, :, 0:1], ctfv, OP.mult))
            if c == 0 and og == 0:
                P.wait("dve", tok_dec)
            first_scan = True
            for o in os_:
                we, vv = weS[o], vvS[o]
                P.do("dve", lambda e, vv=vv, o=o, c=c: e.tensor_tensor_scan(
                    vv[:, :], cdecay[:, o:o + 1].broadcast_to([128, CH]), vv[:, :],
                    0.0 if c == 0 else carry_n[:, o:o + 1], OP.mult, OP.add),
                    nochain=not first_scan)
                first_scan = False
                P.do("dve", lambda e, we=we, o=o, c=c: e.tensor_tensor_scan(
                    we[:, :], cdecay[:, o:o + 1].broadcast_to([128, CH]), we[:, :],
                    0.0 if c == 0 else carry_d[:, o:o + 1], OP.mult, OP.add),
                    nochain=True)
            P.do("dve", lambda e, og=og: e.tensor_copy(
                carry_n[:, og:og + GO].rearrange("p (a b) -> p a b", b=1),
                vvG[:, :, CH - 1:CH]))
            tok_scan = P.do("dve", lambda e, og=og: e.tensor_copy(
                carry_d[:, og:og + GO].rearrange("p (a b) -> p a b", b=1),
                weG[:, :, CH - 1:CH]), nochain=True)
            tok_d2 = P.do("dve", lambda e: e.scalar_tensor_tensor(
                weG, weG, WKV_EPS, sgG, OP.add, OP.mult))
            P.wait("act", tok_d2)
            P.do("act", lambda e: e.activation(weG, weG, AF.Ln))
            tok_rd = P.do("act", lambda e: e.activation(
                weG, weG, AF.Exp, scale=C(-1.0)))
            P.wait("dve", tok_rd)
            tok_rw_last = P.do("dve", lambda e, og=og: e.tensor_tensor(
                A_rwb[:, og:og + GO, :], vvG, weG, OP.mult))
            otr_free["p1"] = tok_rw_last
        p1_prev["pe_after_kvr"] = ("e_pe", P.cnt["e_pe"])

        t0b = t0bufs[c % 2]
        P.wait("pe", tok_rw_last)
        if c == 0:
            P.wait("pe", tw[3])
        tok_t = None
        for op_ in range(0, NCT, 2):
            bkidx = (op_ // 2) % 4
            bk = pM[bkidx]
            P.wait("pe", bank_free[bkidx])
            tok_mm = None
            for half in range(2):
                o = op_ + half
                for k in range(NCT):
                    tok_mm = P.do("pe", lambda e, k=k, o=o, half=half, bk=bk:
                                  e.matmul(
                                      bk[:, half * CH:(half + 1) * CH],
                                      AW1[:, k, 3 * D + o * 128:3 * D + (o + 1) * 128],
                                      A_rwb[:, k, :],
                                      start=(k == 0), stop=(k == NCT - 1)))
            P.wait("dve", tok_mm)
            tok_t = P.do("dve", lambda e, op_=op_, bk=bk, t0b=t0b: e.tensor_tensor(
                t0b[:, op_:op_ + 2, :], t0b[:, op_:op_ + 2, :],
                bk[:, 0:2 * CH].rearrange("p (a b) -> p a b", b=CH), OP.add),
                nochain=(op_ > 0))
            bank_free[bkidx] = tok_t
        p1_prev["tok_t"] = tok_t

    def p1_back_tail(c):
        t0b = t0bufs[c % 2]
        tok_t = p1_prev["tok_t"]
        P.wait("act", tok_t)
        tok_tst = P.dma("act", dts.ap()[c].rearrange("p (a b) -> p a b", a=NCT),
                        t0b[:, :, :])
        if DBG:
            P.wait("pool", tok_t)
            tok_d = P.dma("pool", dbg["dbg_t"].ap()
                          .rearrange("(ct p) t -> p ct t", p=128)
                          [:, :, c * CH:(c + 1) * CH], t0b[:, :, :])
            P.wait("dve", tok_d)

        P.wait_all("pe", [tok_t, pB_free[0], pB_free[1]])
        tok_ms = None
        for ct in range(NCT):
            tok_ms = P.do("pe", lambda e, ct=ct, t0b=t0b: e.matmul(
                pB[0][0:1, 0:CH], ones_col[:, 0:1], t0b[:, ct, :],
                start=(ct == 0), stop=(ct == NCT - 1)))
        P.wait("act", tok_t)
        P.wait("act", otr_free.get("p1"))
        tok_sq = P.do("act", lambda e, t0b=t0b: e.activation(
            A_ot3[:, 4:4 + NCT, :], t0b[:, :, :], AF.Square))
        P.wait("pe", tok_sq)
        tok_qs = None
        for ct in range(NCT):
            tok_qs = P.do("pe", lambda e, ct=ct: e.matmul(
                pB[1][0:1, 0:CH], ones_col[:, 0:1], A_ot3[:, 4 + ct, :],
                start=(ct == 0), stop=(ct == NCT - 1)))
        p1_prev["ln2_qs_pe"] = ("e_pe", P.cnt["e_pe"])
        otr_free["p1"] = p1_prev["ln2_qs_pe"]
        p1_prev[("t0f", c % 2)] = [tok_tst, p1_prev["ln2_qs_pe"], tok_sq]
        P.wait_all("dve", [tok_ms, tok_qs])
        P.wait_all("dve", p1_prev.get("srows_st") or [])
        P.do("dve", lambda e: e.tensor_scalar_mul(scr_rows[0:1, 0:CH],
                                                  pB[0][0:1, 0:CH], inv_d))
        P.do("dve", lambda e: e.tensor_tensor(scr_rows[0:1, CH:2 * CH],
                                              scr_rows[0:1, 0:CH],
                                              scr_rows[0:1, 0:CH], OP.mult))
        tok_vr2 = P.do("dve", lambda e: e.scalar_tensor_tensor(
            scr_rows[0:1, CH:2 * CH], pB[1][0:1, 0:CH], inv_d,
            scr_rows[0:1, CH:2 * CH], OP.mult, OP.subtract))
        pB_free[0] = tok_vr2
        pB_free[1] = tok_vr2
        P.wait("act", tok_vr2)
        P.do("act", lambda e: e.activation(scr_rows[0:1, CH:2 * CH],
                                           scr_rows[0:1, CH:2 * CH],
                                           AF.Ln, bias=C1(LN_EPS)))
        tok_s2 = P.do("act", lambda e: e.activation(
            scr_rows[0:1, CH:2 * CH], scr_rows[0:1, CH:2 * CH],
            AF.Exp, scale=C1(-0.5)))
        P.wait("sp", tok_s2)
        t1 = P.dma("sp", dstat.ap()[0:1, c * CH:(c + 1) * CH],
                   scr_rows[0:1, 0:CH])
        t2_ = P.dma("sp", dstat.ap()[1:2, c * CH:(c + 1) * CH],
                    scr_rows[0:1, CH:2 * CH])
        p1_prev["srows_st"] = [t1, t2_]

    if not cfg.get("skip_p1"):
        p1_front_a(0)
        p1_front_b(0)
        for c in range(NCH):
            p1_back(c)
            if c + 1 < NCH:
                p1_front_b(c + 1)
            p1_back_tail(c)
            emit_zero_copies((n_zdma + NCH - 1) // NCH)
            chunk_f[c % 2] = P.fence_all()
        emit_zero_copies(n_zdma)
    else:
        p1_front_a(0)
        p1_front_b(0)
        p1_back(0)
        p1_back_tail(0)
        chunk_f[0] = P.fence_all()

    f_p1 = P.fence_all()
    P.wait_all("pool", f_p1)
    P.wait_all("sp", f_p1)
    P.wait_all("dve", f_p1)
    P.wait_all("act", f_p1)
    P.wait_all("pe", f_p1)
    tok_ffk = P.dma("pool", AW1[:, :, 0:DF],
                    dWffkT.ap().rearrange("(c p) f -> p c f", p=128))
    tok_cv_w = [P.dma("pool", dst.ap(), srcd.ap())
                for dst, srcd in [(dw1b, dw1), (dw2b, dw2)]]

    tok_repb2 = None
    for ct in range(NCT):
        tok_repb2 = P.do("dve", lambda e, ct=ct: e.tensor_scalar_mul(
            repb[:, ct, :], C(1.0).broadcast_to([128, 16]), cb2[:, ct:ct + 1]))
    P.wait_all("pe", [tok_ffk, tok_ffv, tok_repb2])
    tok_bm2 = None
    for h in range(NFT):
        for ct in range(NCT):
            tok_bm2 = P.do("pe", lambda e, h=h, ct=ct: e.matmul(
                pM[0][:, h * 16:(h + 1) * 16],
                AW1[:, ct, h * 128:(h + 1) * 128],
                repb[:, ct, :], start=(ct == 0), stop=(ct == NCT - 1)))
    P.wait("dve", tok_bm2)
    tok_bc2 = P.do("dve", lambda e: e.tensor_copy(
        bcol2[:, :].rearrange("p (a b) -> p a b", b=2),
        pM[0][:, 0:16 * NFT].rearrange("p (a b) -> p a b", b=16)[:, :, 0:2]))
    bank_free[0] = tok_bc2
    tok_wf2 = None
    for ct in range(NCT):
        tok_wf2 = P.do("dve", lambda e, ct=ct: e.tensor_scalar_mul(
            AW1[:, ct, 0:DF], AW1[:, ct, 0:DF], cg2[:, ct:ct + 1]))
    P.wait("pe", tok_wf2)

    p2s = {}

    def p2_chunk(c):
        pr = c % 2
        xt = A_xtm[pr]
        P.wait("sp", p2s.get("t0_free"))
        if DBG:
            P.wait("sp", p2s.get("dbg_tfin"))
        tok_t = P.dma("sp", A_t0[:, :, :],
                      dts.ap()[c].rearrange("p (a b) -> p a b", a=NCT))
        P.wait("sp", p2s.get("brows_pe"))
        tok_m = P.dma("sp", scr_rows[0:1, 0:CH],
                      dstat.ap()[0:1, c * CH:(c + 1) * CH])
        tok_s2 = P.dma("sp", scr_rows[0:1, CH:2 * CH],
                       dstat.ap()[1:2, c * CH:(c + 1) * CH])
        if c >= 1:
            p2_router(c - 1)
        P.wait_all("pe", [tok_t, tok_m, tok_s2] + pB_free)
        tok_b0 = P.do("pe", lambda e: e.matmul(
            pB[0][:, 0:CH], ones_row[0:1, :], scr_rows[0:1, 0:CH],
            start=True, stop=True))
        tok_b1 = P.do("pe", lambda e: e.matmul(
            pB[1][:, 0:CH], ones_row[0:1, :], scr_rows[0:1, CH:2 * CH],
            start=True, stop=True))
        p2s["brows_pe"] = ("e_pe", P.cnt["e_pe"])
        P.wait_all("dve", [tok_b0, tok_b1, p2s.get("router_pe"),
                           p2s.get("u2_pe"), p2s.get("t2tm_dma")])
        rep0 = pB[0][:, 0:CH].rearrange("p (a b) -> p a b", a=1) \
            .broadcast_to([128, NCT, CH])
        rep1 = pB[1][:, 0:CH].rearrange("p (a b) -> p a b", a=1) \
            .broadcast_to([128, NCT, CH])
        xlb = A_xlb
        xlbf = xlb_flat
        P.do("dve", lambda e, rep0=rep0: e.tensor_tensor(
            A_xlf[:, :, :], A_t0[:, :, :], rep0, OP.subtract))
        tok_xl2 = P.do("dve", lambda e, rep1=rep1, xlb=xlb: e.tensor_tensor(
            xlb[:, :, :], A_xlf[:, :, :], rep1, OP.mult))
        pB_free[0] = tok_xl2
        pB_free[1] = tok_xl2

        P.wait("pe", tok_xl2)
        tok_u2 = None
        for h in range(NFT):
            war = otr_free.get(("p2", h % 2))

            def ev_u(bk, tok_mm, h=h, war=war):
                P.wait("act", tok_mm)
                P.wait("act", war)
                P.wait("act", p2s.get("tfin_pe"))
                tok_rl = P.do("act", lambda e, bk=bk, h=h: e.activation(
                    A_otr[8 + h % 2][:, :], bk[:, 0:CH], AF.Relu,
                    bias=bcol2[:, 2 * h:2 * h + 1]))
                P.wait("dve", tok_rl)
                return P.do("dve", lambda e, h=h: e.tensor_tensor(
                    A_u2[:, h, :], A_otr[8 + h % 2][:, :], A_otr[8 + h % 2][:, :],
                    OP.mult), nochain=(h > 0))
            tok_u2 = mm_group(h % 4,
                              lambda k, h=h: AW1[:, k, h * 128:(h + 1) * 128],
                              lambda k, xlb=xlb: xlb[:, k, :], NCT, ev_u)
            otr_free[("p2", h % 2)] = tok_u2
        p2s["u2_pe"] = ("e_pe", P.cnt["e_pe"])
        P.wait("pe", tok_u2)
        tok_tf = None
        for cop in range(0, NCT, 2):
            bkidx = (cop // 2) % 4
            bk = pM[bkidx]
            P.wait("pe", bank_free[bkidx])
            tok_mm = None
            for half in range(2):
                co = cop + half
                for k in range(NFT):
                    tok_mm = P.do("pe", lambda e, k=k, co=co, half=half, bk=bk:
                                  e.matmul(
                                      bk[:, half * CH:(half + 1) * CH],
                                      AW2[:, k, co * 128:(co + 1) * 128],
                                      A_u2[:, k, :],
                                      start=(k == 0), stop=(k == NFT - 1)))
            P.wait("dve", tok_mm)
            tok_tf = P.do("dve", lambda e, cop=cop, bk=bk: e.tensor_tensor(
                A_t0[:, cop:cop + 2, :], A_t0[:, cop:cop + 2, :],
                bk[:, 0:2 * CH].rearrange("p (a b) -> p a b", b=CH), OP.add),
                nochain=(cop > 0))
            bank_free[bkidx] = tok_tf
        if DBG:
            P.wait("pool", tok_tf)
            tok_d = P.dma("pool", dbg["dbg_tfin"].ap()
                          .rearrange("(ct p) t -> p ct t", p=128)
                          [:, :, c * CH:(c + 1) * CH], A_t0[:, :, :])
            P.wait("dve", tok_d)
            p2s["dbg_tfin"] = tok_d

        p2s["tfin_pe"] = ("e_pe", P.cnt["e_pe"])
        xo = A_xtm[(c + 1) % 2]
        P.wait("act", p2s.get("x2st"))
        P.wait("act", p2s.get("t2b"))
        tok_tm = cm_to_tm(A_t0, xo, tok_tf)
        p2s["t0_free"] = ("e_pe", P.cnt["e_pe"])

        P.wait("act", p2s.get("x2_dve"))
        tok_X = P.dma("act", xt.rearrange("p (tt f) -> p tt f", tt=TTC),
                      dx.ap()[c * CH:(c + 1) * CH, :]
                      .rearrange("(tt p) f -> p tt f", p=128))
        P.wait_all("dve", [tok_X, tok_tm])
        tok_x2 = None
        for tt in range(TTC):
            tok_x2 = P.do("dve", lambda e, tt=tt: e.tensor_tensor(
                xt[:, tt * D:(tt + 1) * D], xt[:, tt * D:(tt + 1) * D],
                xo[:, tt * D:(tt + 1) * D], OP.add), nochain=(tt > 0))
        p2s["x2_dve"] = tok_x2
        P.wait("act", tok_x2)
        p2s["x2st"] = P.dma("act", dX2.ap()[c * CH:(c + 1) * CH, :]
                            .rearrange("(tt p) f -> p tt f", p=128),
                            xt.rearrange("p (tt f) -> p tt f", tt=TTC))
        tok_s0p = ln_stats_tm(xt, False, pre_tok=tok_x2, dump=u2_flat)
        P.wait("dve", tok_s0p)
        tok_t2b = None
        for tt in range(TTC):
            tok_t2b = P.do("dve", lambda e, tt=tt, xlbf=xlbf: e.tensor_scalar(
                xlbf[:, tt * D:(tt + 1) * D], xt[:, tt * D:(tt + 1) * D],
                stat[:, 8 + tt:9 + tt], stat[:, 12 + tt:13 + tt],
                OP.subtract, OP.mult), nochain=(tt > 0))
        p2s["t2b"] = tok_t2b
        P.wait("pool", tok_t2b)
        p2s["t2tm_dma"] = P.dma(
            "pool", dt2tm.ap()[c * CH:(c + 1) * CH, :]
            .rearrange("(tt p) f -> p tt f", p=128),
            xlbf[:, 0:TTC * D].rearrange("p (tt f) -> p tt f", tt=TTC))
        if DBG:
            tok_d = P.dma("pool", dbg["dbg_t2"].ap()[c * CH:(c + 1) * CH, :]
                          .rearrange("(tt p) f -> p tt f", p=128),
                          xlbf[:, 0:TTC * D].rearrange("p (tt f) -> p tt f", tt=TTC))
            P.wait("dve", tok_d)

    nsl = 1024 // CH
    stg8 = A_ot3[:, 0:nsl, :].rearrange("p a (b c) -> p (a b) c", c=128)

    def p2_router(c):
        P.wait("dve", p2s.get("t2cm_dma"))
        P.wait("pe", p2s.get("t2b"))
        tok_cmb = None
        for tt in range(TTC):
            g = c * TTC + tt
            for half in range(2):
                bi = half
                P.wait("pe", pT_free[bi])
                tok_tp3 = None
                for j in range(4):
                    ct = half * 4 + j
                    tok_tp3 = P.do("pe", lambda e, ct=ct, tt=tt, bi=bi, j=j:
                                   e.matmul(
                                       pT[bi][0:128, j * 128:(j + 1) * 128],
                                       xlb_flat[:, tt * D + ct * 128: tt * D + (ct + 1) * 128],
                                       ident_b[:, :], start=True, stop=True))
                P.wait("act", tok_tp3)
                pview = pT[bi][0:128, 0:512].rearrange("p (a b) -> p a b", b=128)
                stg = A_ot3[:, half * (nsl // 2):(half + 1) * (nsl // 2), :]
                tok_rtl = P.do("act", lambda e, stg=stg, pview=pview: e.copy(
                    stg, pview))
                P.wait("dve", tok_rtl)
                tok_cmb = P.do("dve", lambda e, half=half, tt=tt, pview=pview:
                               e.tensor_copy(
                                   A_rwb[:, half * 4:(half + 1) * 4,
                                         tt * 128:(tt + 1) * 128], pview))
                pT_free[bi] = tok_cmb
                P.wait("pe", tok_cmb)
            P.wait("pe", bank_free[0])
            tok_rmm = None
            for ct in range(NCT):
                tok_rmm = P.do("pe", lambda e, ct=ct: e.matmul(
                    pM[0][:, 0:E], stg8[:, ct, :], rw_sb[:, ct, :],
                    start=(ct == 0), stop=(ct == NCT - 1)))
            P.wait("dve", tok_rmm)
            tok_lg = P.do("dve", lambda e: e.tensor_tensor(
                A_otr[6][:, 0:E], pM[0][:, 0:E], rb_b[:, 0:E], OP.add))
            bank_free[0] = tok_lg
            P.do("dve", lambda e: e.max(A_otr[7][:, 0:8], A_otr[6][:, 0:E]))
            P.do("dve", lambda e: e.max_index(
                idx_u[:, 0:8], A_otr[7][:, 0:8], A_otr[6][:, 0:E]))
            tok_eid = P.do("dve", lambda e, g=g: e.tensor_copy(
                eid_tiles[:, g:g + 1], idx_u[:, 0:1]))
        p2s["router_pe"] = ("e_pe", P.cnt["e_pe"])
        p2s["router_dve"] = tok_eid
        P.wait("sp", tok_cmb)
        p2s["t2cm_dma"] = P.dma(
            "sp", dt2cm.ap()[c].rearrange("p (a b) -> p a b", a=NCT),
            A_rwb[:, :, :])

    NP2 = NCH if not cfg.get("skip_p2") else 1
    for c in range(NP2):
        p2_chunk(c)
    p2_router(NP2 - 1)

    f_p2 = P.fence_all()
    P.wait_all("sp", f_p2)
    P.wait_all("dve", f_p2)
    P.wait_all("act", f_p2)

    P.wait_all("pe", f_p2)
    P.wait_all("pool", f_p2)

    P.wait_all("act", tok_cv)
    P.wait_all("sp", tok_cv)
    tok_ow1 = P.dma("act", AW1[:, :, 0:DF],
                    dow1b.ap().rearrange("(c p) f -> p c f", p=128))
    tok_ow2 = P.dma("sp", AW2[:, :, :],
                    dow2b.ap().rearrange("(h p) f -> p h f", p=128))
    p34 = {}

    def yov_chunk(c, phase="all"):
        if phase in ("all", "uov"):
            P.wait("sp", p34.get("rwb_pe"))
            P.wait("sp", p34.get("u2_pe"))
            tok_t2 = P.dma("sp", A_rwb[:, :, :],
                           dt2cm.ap()[c].rearrange("p (a b) -> p a b", a=NCT))
            P.wait_all("pe", [tok_t2, tok_ow1, p34.get("u2_pe")])
            tok_u = None
            for h in range(NFT):
                def ev_uo(bk, tok_mm, h=h):
                    P.wait("act", tok_mm)
                    P.wait("act", p34.get("u2_pe"))
                    return P.do("act", lambda e, bk=bk, h=h: e.activation(
                        A_u2[:, h, :], bk[:, 0:CH], AF.Relu,
                        bias=cob1[:, h:h + 1]), nochain=(h > 0))
                tok_u = mm_group(h % 4,
                                 lambda k, h=h: AW1[:, k, h * 128:(h + 1) * 128],
                                 lambda k: A_rwb[:, k, :], NCT, ev_uo)
            p34["rwb_pe"] = ("e_pe", P.cnt["e_pe"])
            p34[("mid", c)] = tok_u
            if phase == "uov":
                return
        tok_u = p34.pop(("mid", c))
        tok_iv = P.dma("sp", A_otr[6][0:1, 0:CH],
                       divm.ap().rearrange("(a b) -> a b", a=1)
                       [0:1, c * CH:(c + 1) * CH])
        P.wait_all("pe", [tok_iv] + pB_free)
        tok_ivb = P.do("pe", lambda e: e.matmul(
            pB[0][:, 0:CH], ones_row[0:1, :], A_otr[6][0:1, 0:CH],
            start=True, stop=True))
        P.wait("dve", tok_ivb)
        tok_ivs = P.do("dve", lambda e: e.tensor_copy(A_otr[7][:, :],
                                                      pB[0][:, 0:CH]))
        pB_free[0] = tok_ivs
        P.wait_all("pe", [tok_u, tok_ow2])
        tok_zc = None
        for co in range(NCT):
            def ev_z(bk, tok_mm, co=co):
                P.wait_all("dve", [tok_mm, p34.get("zst")])
                P.do("dve", lambda e, bk=bk, co=co: e.tensor_scalar_add(
                    A_t0[:, co, :], bk[:, 0:CH], cob2[:, co:co + 1]),
                    nochain=(co > 0))
                return P.do("dve", lambda e, co=co: e.tensor_tensor(
                    A_t0[:, co, :], A_t0[:, co, :], A_otr[7][:, :], OP.mult))
            tok_zc = mm_group(co % 4,
                              lambda k, co=co: AW2[:, k, co * 128:(co + 1) * 128],
                              lambda k: A_u2[:, k, :], NFT, ev_z)
        p34["u2_pe"] = ("e_pe", P.cnt["e_pe"])
        P.wait("act", tok_zc)
        p34["zst"] = P.dma("act", dts.ap()[c].rearrange("p (a b) -> p a b", a=NCT),
                           A_t0[:, :, :])
        if DBG:
            P.wait("pool", tok_zc)
            tok_dz = P.dma("pool", dbg["dbg_z"].ap()
                           .rearrange("(ct p) t -> p ct t", p=128)
                           [:, :, c * CH:(c + 1) * CH], A_t0[:, :, :])
            P.wait("dve", tok_dz)
            if c == 0:
                P.wait("pool", tok_u)
                tok_du = P.dma("pool", dbg["dbg_uov"].ap()
                               .rearrange("(h p) t -> p h t", p=128),
                               A_u2[:, :, 0:CH])
                P.wait("act", tok_du)

    P.wait("pe", pT_free[0])
    tok_te = P.do("pe", lambda e: e.transpose(
        pT[0][0:NTT, 0:128], eid_tiles[:, 0:NTT], ident_f[:, :]))
    P.wait("act", tok_te)
    tok_eidr = P.do("act", lambda e: e.copy(A_dump[0:NTT, 0:128],
                                            pT[0][0:NTT, 0:128]))
    pT_free[0] = tok_eidr
    P.wait("sp", tok_eidr)
    tok_eidd = P.dma("sp", deid.ap().rearrange("(a b) -> a b", b=128),
                     A_dump[0:NTT, 0:128])
    if DBG:
        P.wait("pool", tok_eidd)
        P.dma("pool", dbg["dbg_eid"].ap(), deid.ap())

    HW = T // 2
    NQ = (HW + 511) // 512
    t0row = A_t0[:, :, :].rearrange("p a b -> p (a b)")
    ohb = A_xtm[0]
    cumb = A_xtm[1]
    onesEb = cst[0:E, CVALS.index(1.0):CVALS.index(1.0) + 1]

    def oh_half(h, tok_eid_row):
        P.wait_all("pe", [tok_eid_row] + [bank_free[q % 4] for q in range(NQ)])
        mm = []
        for q in range(NQ):
            qw = min(512, HW - q * 512)
            mm.append(P.do("pe", lambda e, q=q, qw=qw: e.matmul(
                pM[q % 4][0:E, 0:qw], ones_row[0:1, 0:E],
                t0row[0:1, q * 512:q * 512 + qw], start=True, stop=True)))
        tok_oh = None
        for q in range(NQ):
            qw = min(512, HW - q * 512)
            P.wait("dve", mm[q])
            tok_oh = P.do("dve", lambda e, q=q, qw=qw: e.tensor_scalar(
                ohb[0:E, q * 512:q * 512 + qw], pM[q % 4][0:E, 0:qw],
                iota8[0:E, 0:1], None, OP.is_equal))
            bank_free[q % 4] = tok_oh
        return tok_oh

    P.wait("sp", tok_eidd)
    tok_rloc_st = []
    tok_eidr2 = None
    for h in range(2):
        P.wait("sp", tok_eidr2)
        tok_er = P.dma("sp", t0row[0:1, 0:HW],
                       deid.ap().rearrange("(a b) -> a b", a=1)
                       [0:1, h * HW:(h + 1) * HW])
        tok_oh = oh_half(h, tok_er)
        tok_eidr2 = ("e_pe", P.cnt["e_pe"])
        tok_cum = P.do("dve", lambda e, h=h: e.tensor_tensor_scan(
            cumb[0:E, 0:HW], onesEb.broadcast_to([E, HW]), ohb[0:E, 0:HW],
            0.0 if h == 0 else carry8[0:E, 0:1], OP.mult, OP.add))
        P.do("dve", lambda e: e.tensor_copy(
            carry8[0:E, 0:1], cumb[0:E, HW - 1:HW]))
        tok_m = P.do("dve", lambda e: e.tensor_mul(
            cumb[0:E, 0:HW], cumb[0:E, 0:HW], ohb[0:E, 0:HW]))
        P.wait_all("pe", [tok_m, pB_free[1]])
        P.wait_all("dve", tok_rloc_st)
        tok_rl = None
        for q in range(NQ):
            qw = min(512, HW - q * 512)
            tok_cs = P.do("pe", lambda e, q=q, qw=qw: e.matmul(
                pB[1][0:1, 0:qw], ones_col[0:E, 0:1],
                cumb[0:E, q * 512:q * 512 + qw], start=True, stop=True))
            P.wait("dve", tok_cs)
            tok_rl = P.do("dve", lambda e, q=q, qw=qw: e.tensor_copy(
                A_dump[0:1, q * 512:q * 512 + qw], pB[1][0:1, 0:qw]))
            P.wait("pe", tok_rl)
        pB_free[1] = tok_rl
        P.wait("sp", tok_rl)
        tok_rloc_st.append(P.dma("sp", dstat.ap()[0:1, h * HW:(h + 1) * HW],
                                 A_dump[0:1, 0:HW]))
    tok_cntd = P.dma("sp", dcnt.ap().rearrange("(a b) -> a b", b=1), carry8[0:E, 0:1])
    P.wait_all("pool", [tok_cntd])
    P.do("pool", lambda e: e.collective_compute(
        "AllGather", OP.bypass, replica_groups=[list(range(n_cores))],
        ins=[dcnt.ap().opt()], outs=[dcall.ap().opt()]), inc="c_cnt")

    P.wait("pool", ("c_cnt", 1))
    tok_call = P.dma("pool", call_sb[0:n_cores, 0:E],
                     dcall.ap().rearrange("(r e) -> r e", e=E))
    if DBG:
        P.dma("pool", dbg["dbg_cnt"].ap(), dcnt.ap())
        P.dma("pool", dbg["dbg_call"].ap(), dcall.ap())
    P.wait_all("pe", [tok_call, tok_pfx] + pB_free)
    if DBG:
        P.wait_all("pool", [tok_call, tok_pfx])
        P.dma("pool", dbg["dbg_pfx"].ap(), pfx_sb[0:8, 0:1])
        P.dma("pool", dbg["dbg_csb"].ap(), call_sb[0:8, 0:E])
    tok_pb = P.do("dve", lambda e: e.tensor_scalar_mul(
        A_otr[7][0:8, 0:CH],
        cst[0:8, CVALS.index(1.0):CVALS.index(1.0) + 1].broadcast_to([8, CH]),
        pfx_sb[0:8, 0:1]))
    P.wait("pe", tok_pb)
    tok_bmm = P.do("pe", lambda e: e.matmul(
        pB[0][0:E, 0:CH], call_sb[0:8, 0:E], A_otr[7][0:8, 0:CH],
        start=True, stop=True))
    P.wait("dve", tok_bmm)
    if DBG:
        tok_dd = P.do("dve", lambda e: e.tensor_copy(A_otr[6][0:8, 0:16],
                                                     pB[0][0:E, 0:16]))
        P.wait("pool", tok_dd)
        P.dma("pool", dbg["dbg_pbc"].ap(), A_otr[7][0:8, 0:16])
        tok_dd2 = P.dma("pool", dbg["dbg_brow"].ap(), A_otr[6][0:8, 0:16])
        P.wait("dve", tok_dd2)
    P.do("dve", lambda e: e.tensor_copy(base8[0:E, 0:2], pB[0][0:E, 0:2]))
    tok_b8 = P.do("dve", lambda e: e.tensor_scalar_add(base8[0:E, 1:2],
                                                       base8[0:E, 0:1], -1.0))
    pB_free[0] = tok_b8

    QW = min(512, HW)
    ddf_all = []
    wq_tok = list(tok_rloc_st)
    f_pass1 = ("e_pe", P.cnt["e_pe"])
    prev_dst = None
    for h in range(2):
        P.wait_all("sp", [tok_eidr2, f_pass1, prev_dst])
        tok_er = P.dma("sp", t0row[0:1, 0:HW],
                       deid.ap().rearrange("(a b) -> a b", a=1)
                       [0:1, h * HW:(h + 1) * HW])
        tok_rll = P.dma("sp", cumb[0:1, 0:HW],
                        dstat.ap()[0:1, h * HW:(h + 1) * HW])
        tok_oh = oh_half(h, tok_er)
        tok_eidr2 = ("e_pe", P.cnt["e_pe"])
        P.wait_all("pe", [tok_oh, tok_b8, pB_free[1]])
        for q in range(NQ):
            qw = min(512, HW - q * 512)
            P.wait("pe", pB_free[1])
            tok_bs = P.do("pe", lambda e, q=q, qw=qw: e.matmul(
                pB[1][0:1, 0:qw], base8[0:E, 1:2],
                ohb[0:E, q * 512:q * 512 + qw], start=True, stop=True))
            P.wait_all("dve", [tok_bs, tok_rll] + wq_tok)
            wq_tok = []
            P.do("dve", lambda e, q=q, qw=qw: e.tensor_tensor(
                A_dump[0:1, 0:qw], cumb[0:1, q * 512:q * 512 + qw],
                pB[1][0:1, 0:qw], OP.add))
            P.do("dve", lambda e, qw=qw: e.tensor_scalar(
                A_dump[0:1, 2 * QW:2 * QW + qw], A_dump[0:1, 0:qw], float(CAP),
                -1.0, OP.is_lt, OP.mult))
            tok_ivm = P.do("dve", lambda e, qw=qw: e.tensor_scalar_add(
                A_dump[0:1, 2 * QW:2 * QW + qw], A_dump[0:1, 2 * QW:2 * QW + qw], 1.0))
            P.do("dve", lambda e, q=q, qw=qw: e.scalar_tensor_tensor(
                A_dump[0:1, QW:QW + qw], t0row[0:1, q * 512:q * 512 + qw],
                float(CAP), A_dump[0:1, 0:qw], OP.mult, OP.add))
            tok_dst = P.do("dve", lambda e, qw=qw: e.scalar_tensor_tensor(
                A_dump[0:1, QW:QW + qw], A_dump[0:1, 2 * QW:2 * QW + qw], BIG,
                A_dump[0:1, QW:QW + qw], OP.mult, OP.add))
            pB_free[1] = tok_dst
            prev_dst = tok_dst
            P.wait("sp", tok_ivm)
            P.wait("sp", tok_dst)
            t1_ = P.dma("sp", divm.ap().rearrange("(a b) -> a b", a=1)
                        [0:1, h * HW + q * 512:h * HW + q * 512 + qw],
                        A_dump[0:1, 2 * QW:2 * QW + qw])
            t2__ = P.dma("sp", ddf.ap().rearrange("(a b) -> a b", a=1)
                         [0:1, h * HW + q * 512:h * HW + q * 512 + qw],
                         A_dump[0:1, QW:QW + qw])
            ddf_all.extend([t1_, t2__])
            wq_tok = [t1_, t2__]
            if DBG:
                P.wait("pool", tok_dst)
                tok_d = P.dma("pool", dbg["dbg_rank"].ap()
                              .rearrange("(a b) -> a b", a=1)
                              [0:1, h * HW + q * 512:h * HW + q * 512 + qw],
                              A_dump[0:1, 0:qw])
                P.wait("dve", tok_d)
                wq_tok.append(tok_d)
                ddf_all.append(tok_d)

    P.wait_all("sp", ddf_all)
    tok_dl = P.dma("sp", A_dump[0:NTT, 0:128],
                   ddf.ap().rearrange("(a b) -> a b", b=128))
    P.wait_all("pe", [tok_dl, pT_free[0]])
    tok_dtr = P.do("pe", lambda e: e.transpose(
        pT[0][0:128, 0:NTT], A_dump[0:NTT, 0:128], ident_f[0:NTT, 0:NTT]))
    P.wait("dve", tok_dtr)
    tok_di = P.do("dve", lambda e: e.tensor_copy(desti_t[:, 0:NTT],
                                                 pT[0][0:128, 0:NTT]))
    pT_free[0] = tok_di
    if DBG:
        P.wait("pool", tok_di)
        P.dma("pool", dbg["dbg_dest"].ap(), ddf.ap())
        P.dma("pool", dbg["dbg_ivm"].ap(), divm.ap())

    emit_zero_copies(n_zdma)
    P.wait_all("pool", [tok_di] + zfill_done)
    P.wait_all("pool", P.dma_fence("pool"))
    P.wait_all("pool", ddf_all)
    if NCT * CH >= 2 * D:
        scat_buf = [xlb_flat[:, 0:D], xlb_flat[:, D:2 * D]]
    else:
        scat_buf = [xlb_flat[:, 0:D], xlb_flat[:, 0:D]]
    scat_tok = [None, None]
    for g in range(0 if cfg.get("skip_scat") else NTT):
        sbuf_g = scat_buf[g % 2]
        P.wait_all("pool", [scat_tok[g % 2]])
        tok_ld = P.dma("pool", sbuf_g,
                       dt2tm.ap()[g * 128:(g + 1) * 128, :])
        P.wait("pool", tok_ld)
        scat_tok[g % 2] = P.idma("pool", lambda e, g=g, sbuf_g=sbuf_g:
                                 e.indirect_dma_start(
                                     dslot.ap(),
                                     IndirectOffsetOnAxis(desti_t[:, g:g + 1], 0),
                                     sbuf_g, None,
                                     bounds_check=rbnds[0],
                                     oob_is_err=False))
    P.wait_all("pool", P.dma_fence("pool"))
    P.do("pool", lambda e: e.collective_compute(
        "ReduceScatter", OP.add, replica_groups=[list(range(n_cores))],
        ins=[dslot.ap().opt()], outs=[dmybuf.ap().opt()]), inc="c_rs")

    for c in range(NCH):
        yov_chunk(c)

    P.wait("act", p34.get("rwb_pe"))
    P.wait_all("act", tok_cv_w)
    P.wait_all("sp", tok_cv_w)
    tok_w1 = P.dma("act", AW1[:, :, 0:DF],
                   dw1b.ap().rearrange("(c p) f -> p c f", p=128))
    P.wait("sp", p34.get("u2_pe"))
    tok_w2 = P.dma("sp", AW2[:, :, :],
                   dw2b.ap().rearrange("(h p) f -> p h f", p=128))
    P.wait("pool", ("c_rs", 1))
    if DBG:
        P.dma("pool", dbg["dbg_buf"].ap(), dmybuf.ap())
    P.wait_all("pe", [tok_w1, tok_w2])

    RCB = TTC * 128
    nrcb = (CAP + RCB - 1) // RCB
    tok_yst = None
    for rb in range(nrcb):
        r0 = rb * RCB
        nr = min(RCB, CAP - r0)
        ntile = (nr + 127) // 128
        P.wait_all("pool", P.fence("pe"))
        P.wait_all("pool", P.fence("act"))
        P.wait("act", tok_yst)
        tok_bl = P.dma("pool", A_xtm[0][:, 0:ntile * D]
                       .rearrange("p (tt f) -> p tt f", tt=ntile),
                       dmybuf.ap()[r0:r0 + nr, :]
                       .rearrange("(tt p) f -> p tt f", p=128))
        P.wait("pe", tok_bl)
        for ct in range(NCT):
            bi = ct % 2
            P.wait("pe", pT_free[bi])
            tok_tr = None
            for tt2 in range(ntile):
                tok_tr = P.do("pe", lambda e, ct=ct, tt2=tt2, bi=bi: e.transpose(
                    pT[bi][0:128, tt2 * 128:(tt2 + 1) * 128],
                    A_xtm[0][:, tt2 * D + ct * 128: tt2 * D + (ct + 1) * 128],
                    ident_f[:, :]))
            P.wait("dve", tok_tr)
            tok_bcm = P.do("dve", lambda e, ct=ct, bi=bi, nr=nr: e.tensor_copy(
                A_rwb[:, ct, 0:nr], pT[bi][0:128, 0:nr]))
            pT_free[bi] = tok_bcm
            P.wait("pe", tok_bcm)
        tok_h = None
        for h in range(NFT):
            def ev_h(bk, tok_mm, h=h, nr=nr):
                P.wait("act", tok_mm)
                return P.do("act", lambda e, bk=bk, h=h, nr=nr: e.activation(
                    A_u2[:, h, 0:nr], bk[:, 0:nr], AF.Relu,
                    bias=cb1e[:, h:h + 1]), nochain=(h > 0))
            tok_h = mm_group(h % 4,
                             lambda k, h=h: AW1[:, k, h * 128:(h + 1) * 128],
                             lambda k, nr=nr: A_rwb[:, k, 0:nr], NCT, ev_h, N=nr)
        P.wait("pe", tok_h)
        tok_yev = None
        for co in range(NCT):
            def ev_y(bk, tok_mm, co=co, nr=nr):
                P.wait("dve", tok_mm)
                return P.do("dve", lambda e, bk=bk, co=co, nr=nr: e.tensor_scalar_add(
                    A_otr[co % 2][:, 0:nr], bk[:, 0:nr], cb2e[:, co:co + 1]),
                    nochain=(co > 1))
            tok_y = mm_group(co % 4,
                             lambda k, co=co: AW2[:, k, co * 128:(co + 1) * 128],
                             lambda k, nr=nr: A_u2[:, k, 0:nr], NFT, ev_y, N=nr)
            P.wait("pe", tok_y)
            bi = co % 2
            P.wait("pe", pT_free[bi])
            tok_ytr = None
            for tt2 in range(ntile):
                tok_ytr = P.do("pe", lambda e, co=co, tt2=tt2, bi=bi: e.transpose(
                    pT[bi][0:128, tt2 * 128:(tt2 + 1) * 128],
                    A_otr[co % 2][:, tt2 * 128:(tt2 + 1) * 128], ident_f[:, :]))
            P.wait("act", tok_ytr)
            for tt2 in range(ntile):
                tok_yev = P.do("act", lambda e, co=co, tt2=tt2, bi=bi: e.copy(
                    A_xtm[1][:, tt2 * D + co * 128: tt2 * D + (co + 1) * 128],
                    pT[bi][0:128, tt2 * 128:(tt2 + 1) * 128]))
            pT_free[bi] = tok_yev
        P.wait("sp", tok_yev)
        tok_yst = P.dma("sp", dytm.ap()[r0:r0 + nr, :]
                        .rearrange("(tt p) f -> p tt f", p=128),
                        A_xtm[1][:, 0:ntile * D]
                        .rearrange("p (tt f) -> p tt f", tt=ntile))
    if DBG:
        P.wait_all("pool", P.dma_fence("sp"))
        P.dma("pool", dbg["dbg_y"].ap(), dytm.ap())

    P.wait_all("pool", P.dma_fence("sp"))
    P.do("pool", lambda e: e.collective_compute(
        "AllGather", OP.bypass, replica_groups=[list(range(n_cores))],
        ins=[dytm.ap().opt()], outs=[dyall.ap().opt()]), inc="c_yag")
    P.wait("pool", ("c_yag", 1))
    tok_yag_done = ("c_yag", P.cnt["c_yag"])

    chunk_f[0] = P.fence_all()
    chunk_f[1] = P.fence_all()

    p4s = {}

    def p4_chunk(c):
        pr = c % 2
        xt = A_xtm[pr]
        if c == 0:
            P.wait_all("sp", P.dma_fence("sp"))
            P.wait_all("sp", P.dma_fence("act"))
            P.wait("act", tok_yst)
        P.wait("sp", p4s.get("t0_free"))
        tok_zl = P.dma("sp", A_t0[:, :, :],
                       dts.ap()[c].rearrange("p (a b) -> p a b", a=NCT))
        xo = A_xtm[(c + 1) % 2]
        P.wait("act", p4s.get("out_dma"))
        P.wait("act", p4s.get("fin_dve"))
        tok_zm = cm_to_tm(A_t0, xo, tok_zl)
        p4s["t0_free"] = ("e_pe", P.cnt["e_pe"])

        P.wait("act", p4s.get("fin_dve"))
        tok_X2 = P.dma("act", xt.rearrange("p (tt f) -> p tt f", tt=TTC),
                       dX2.ap()[c * CH:(c + 1) * CH, :]
                       .rearrange("(tt p) f -> p tt f", p=128))
        P.wait("pool", tok_yag_done)
        P.wait("pool", p4s.get("fin_dve"))
        P.do("pool", lambda e: e.memset(xlf_flat[:, 0:TTC * D], 0.0))
        toks_g = []
        gv = cfg.get("gvariant", 0)
        for tt in range(TTC):
            g = c * TTC + tt
            if gv == 1:
                toks_g.append(P.idma("pool", lambda e, g=g, tt=tt: e.indirect_dma_start(
                    xlf_flat[:, tt * D:(tt + 1) * D], None,
                    dyall.ap(), IndirectOffsetOnAxis(desti_t[:, 0:1], 0),
                    bounds_check=rbnds[0], oob_is_err=False)))
            elif gv == 2:
                toks_g.append(P.idma("pool", lambda e, g=g, tt=tt: e.indirect_dma_start(
                    xlb_flat[:, 0:D], None,
                    dslot.ap(), IndirectOffsetOnAxis(desti_t[:, g:g + 1], 0),
                    bounds_check=rbnds[0], oob_is_err=False)))
            else:
                toks_g.append(P.idma("pool", lambda e, g=g, tt=tt: e.indirect_dma_start(
                    xlf_flat[:, tt * D:(tt + 1) * D], None,
                    dyall.ap(), IndirectOffsetOnAxis(desti_t[:, g:g + 1], 0),
                    bounds_check=rbnds[0], oob_is_err=False)))
        P.wait_all("dve", [tok_X2, tok_zm] + toks_g)
        tok_fin = None
        for tt in range(TTC):
            P.do("dve", lambda e, tt=tt, xo=xo: e.tensor_tensor(
                xt[:, tt * D:(tt + 1) * D], xt[:, tt * D:(tt + 1) * D],
                xo[:, tt * D:(tt + 1) * D], OP.add), nochain=(tt > 0))
            tok_fin = P.do("dve", lambda e, tt=tt: e.tensor_tensor(
                xt[:, tt * D:(tt + 1) * D], xt[:, tt * D:(tt + 1) * D],
                xlf_flat[:, tt * D:(tt + 1) * D], OP.add))
        p4s["fin_dve"] = tok_fin
        P.wait("sp", tok_fin)
        p4s["out_dma"] = P.dma("sp", dout.ap()[c * CH:(c + 1) * CH, :]
                               .rearrange("(tt p) f -> p tt f", p=128),
                               xt.rearrange("p (tt f) -> p tt f", tt=TTC))

    for c in range(NCH):
        p4_chunk(c)

    P.wait_all("pool", P.fence_all())

    with stack:
        with ExitStack() as semstack:
            for s in list(P.cnt.keys()):
                if s.startswith("n_"):
                    continue
                P.sems[s] = semstack.enter_context(nc.semaphore(s))
            with nc.Block() as block:
                @block.tensor
                def _(e):
                    for f in P.streams["pe"]:
                        f(e)

                @block.vector
                def _(e):
                    for f in P.streams["dve"]:
                        f(e)

                @block.scalar
                def _(e):
                    for f in P.streams["act"]:
                        f(e)

                @block.gpsimd
                def _(e):
                    for f in P.streams["pool"]:
                        f(e)

                @block.sync
                def _(e):
                    for f in P.streams["sp"]:
                        f(e)
    return nc



def _prep_inputs(inputs, cfg):
    E = cfg["E"]
    n = cfg["n_cores"]
    x = np.asarray(inputs["L0_attn_x"])
    tr = lambda a: np.ascontiguousarray(np.asarray(a).T)
    cc = lambda a: np.ascontiguousarray(np.asarray(a))
    shared = dict(
        WkT=tr(inputs["L0_attn_Wk"]), WvT=tr(inputs["L0_attn_Wv"]),
        WrT=tr(inputs["L0_attn_Wr"]), WoT=tr(inputs["L0_attn_Wo"]),
        WffkT=tr(inputs["L0_attn_Wffk"]), WffvT=tr(inputs["L0_attn_Wffv"]),
        td=cc(inputs["L0_attn_time_decay"]).reshape(-1),
        tf=cc(inputs["L0_attn_time_first"]).reshape(-1),
        ln1g=cc(inputs["L0_attn_ln1_g"]), ln1b=cc(inputs["L0_attn_ln1_b"]),
        ln2g=cc(inputs["L0_attn_ln2_g"]), ln2b=cc(inputs["L0_attn_ln2_b"]),
        router_w=cc(inputs["L0_ffn_router_w"]),
        router_b=cc(inputs["L0_ffn_router_b"]),
        ow1=cc(inputs["L0_ffn_ow1"]), ob1=cc(inputs["L0_ffn_ob1"]),
        ow2=cc(inputs["L0_ffn_ow2"]), ob2=cc(inputs["L0_ffn_ob2"]),
    )
    in_maps = []
    for i in range(n):
        m = dict(shared)
        m["x"] = cc(x[i])
        e = i % E
        m["w1e"] = cc(inputs["L0_ffn_w1"][e])
        m["b1e"] = cc(inputs["L0_ffn_b1"][e])
        m["w2e"] = cc(inputs["L0_ffn_w2"][e])
        m["b2e"] = cc(inputs["L0_ffn_b2"][e])
        m["prefix_mask"] = (np.arange(8) < i).astype(np.float32)
        in_maps.append(m)
    return in_maps


def kernel(**inputs):
    cfg = make_cfg()
    nc = build_nc(cfg)
    in_maps = _prep_inputs(inputs, cfg)
    res = run_bass_kernel_spmd(nc, in_maps, core_ids=list(range(cfg["n_cores"])))
    out = np.stack([res.results[i]["out"] for i in range(cfg["n_cores"])], axis=0)
    return out.astype(np.float32)


# version 31
# speedup vs baseline: 1.0313x; 1.0313x over previous
import math
import numpy as np
import concourse.bass as bass
import concourse.mybir as mybir
from concourse.bass_utils import run_bass_kernel_spmd
from concourse.bass_core_rust import IndirectOffsetOnAxis

F32 = mybir.dt.float32
BF16 = mybir.dt.bfloat16
I32 = mybir.dt.int32
AF = mybir.ActivationFunctionType
OP = mybir.AluOpType
AX = mybir.AxisListType

LN_EPS = 1e-5
WKV_EPS = 1e-6
EXP_CLAMP = float(math.exp(30.0))
BIG = 1.0e9

D = 1024
NCT = D // 128


def make_cfg(T=4096, DF=4096, CAP=1280, E=8, chunk=256, debug=False, n_cores=8):
    assert T % chunk == 0 and chunk % 128 == 0
    assert CAP % 128 == 0
    return dict(T=T, DF=DF, CAP=CAP, E=E, chunk=chunk, debug=debug,
                n_cores=n_cores, nchunk=T // chunk, tt=chunk // 128,
                nft=DF // 128)


class Prog:

    ENGINES = ("pe", "dve", "act", "pool", "sp")
    AUTO = ("dve", "act", "pool")
    MAX_OUTSTANDING = 24
    DMA_SLOTS = 8
    DMA_SLOTS_BY_ENG = {"pool": 1}

    def __init__(self):
        self.streams = {e: [] for e in self.ENGINES}
        self.cnt = {}
        self.sems = {}
        self.last_wait = {}

    def do(self, eng, fn, inc=None, inc_by=1, nochain=False):
        if eng == "pe" and inc is None:
            inc = "e_pe"
        if eng in self.AUTO:
            s = "e_" + eng
            prev = self.cnt.get(s, 0)
            if prev > 0 and not nochain:
                self.last_wait[(eng, s)] = max(self.last_wait.get((eng, s), 0), prev)
                self.streams[eng].append(
                    lambda e, s=s, v=prev: e.wait_ge(self.sems[s], v))
            if inc is None or inc == s:
                self.cnt[s] = prev + 1
                tok = (s, prev + 1)

                def wrapped(e, fn=fn, s=s):
                    fn(e).then_inc(self.sems[s], 1)
                self.streams[eng].append(wrapped)
                return tok
        tok = None
        if inc is not None:
            self.cnt.setdefault(inc, 0)
            nxt = self.cnt[inc] + inc_by
            lw = self.last_wait.get((eng, inc), 0)
            if nxt - lw > self.MAX_OUTSTANDING and inc_by == 1:
                v = nxt - self.MAX_OUTSTANDING // 2
                self.streams[eng].append(
                    lambda e, s=inc, v=v: e.wait_ge(self.sems[s], v))
                self.last_wait[(eng, inc)] = v
            self.cnt[inc] = nxt
            tok = (inc, nxt)

            def wrapped(e, fn=fn, s=inc, k=inc_by):
                fn(e).then_inc(self.sems[s], k)
            self.streams[eng].append(wrapped)
        else:
            self.streams[eng].append(fn)
        return tok

    def _dma_slot(self, eng):
        nq = "n_" + eng
        n = self.cnt.get(nq, 0)
        self.cnt[nq] = n + 1
        nslots = self.DMA_SLOTS_BY_ENG.get(eng, self.DMA_SLOTS)
        q = f"q_{eng}{n % nslots}"
        uses = self.cnt.get(q, 0)
        if uses > 0:
            self.last_wait[(eng, q)] = max(self.last_wait.get((eng, q), 0), uses)
            self.streams[eng].append(
                lambda e, s=q, v=uses: e.wait_ge(self.sems[s], v))
        return q

    def dma(self, eng, out, in_):
        q = self._dma_slot(eng)
        return self.do(eng, lambda e, o=out, i=in_: e.dma_start(out=o, in_=i),
                       inc=q, inc_by=16)

    def idma(self, eng, fn):
        q = self._dma_slot(eng)
        return self.do(eng, fn, inc=q, inc_by=16)

    def dma_fence(self, eng):
        toks = []
        for s in range(self.DMA_SLOTS):
            q = f"q_{eng}{s}"
            if self.cnt.get(q, 0):
                toks.append((q, self.cnt[q]))
        return toks

    def fence(self, eng):
        toks = list(self.dma_fence(eng))
        s = "e_" + eng
        if self.cnt.get(s, 0):
            toks.append((s, self.cnt[s]))
        return toks

    def fence_all(self):
        return [t for e in self.ENGINES for t in self.fence(e)]

    def wait(self, eng, tok):
        if tok is None:
            return
        s, v = tok
        if self.last_wait.get((eng, s), -1) >= v:
            return
        self.last_wait[(eng, s)] = v
        self.streams[eng].append(lambda e, s=s, v=v: e.wait_ge(self.sems[s], v))

    def wait_all(self, eng, toks):
        best = {}
        for t in toks:
            if t is None:
                continue
            s, v = t
            best[s] = max(best.get(s, 0), v)
        for s, v in sorted(best.items()):
            self.wait(eng, (s, v))


def build_nc(cfg):
    T, DF, CAP, E = cfg["T"], cfg["DF"], cfg["CAP"], cfg["E"]
    CH, NCH, TTC, NFT = cfg["chunk"], cfg["nchunk"], cfg["tt"], cfg["nft"]
    DBG = cfg["debug"]
    n_cores = cfg["n_cores"]
    NTT = T // 128
    inv_d = 1.0 / D

    nc = bass.Bass("TRN2", target_bir_lowering=False)
    P = Prog()

    EI = dict(kind="ExternalInput")
    dx = nc.dram_tensor("x", [T, D], F32, **EI)
    dWkT = nc.dram_tensor("WkT", [D, D], F32, **EI)
    dWvT = nc.dram_tensor("WvT", [D, D], F32, **EI)
    dWrT = nc.dram_tensor("WrT", [D, D], F32, **EI)
    dWoT = nc.dram_tensor("WoT", [D, D], F32, **EI)
    dWffkT = nc.dram_tensor("WffkT", [D, DF], F32, **EI)
    dWffvT = nc.dram_tensor("WffvT", [DF, D], F32, **EI)
    dtd = nc.dram_tensor("td", [D], F32, **EI)
    dtf = nc.dram_tensor("tf", [D], F32, **EI)
    dg1 = nc.dram_tensor("ln1g", [D], F32, **EI)
    db1 = nc.dram_tensor("ln1b", [D], F32, **EI)
    dg2 = nc.dram_tensor("ln2g", [D], F32, **EI)
    db2 = nc.dram_tensor("ln2b", [D], F32, **EI)
    drw = nc.dram_tensor("router_w", [D, E], F32, **EI)
    drb = nc.dram_tensor("router_b", [E], F32, **EI)
    dw1 = nc.dram_tensor("w1e", [D, DF], F32, **EI)
    db1e = nc.dram_tensor("b1e", [DF], F32, **EI)
    dw2 = nc.dram_tensor("w2e", [DF, D], F32, **EI)
    db2e = nc.dram_tensor("b2e", [D], F32, **EI)
    dow1 = nc.dram_tensor("ow1", [D, DF], F32, **EI)
    dob1 = nc.dram_tensor("ob1", [DF], F32, **EI)
    dow2 = nc.dram_tensor("ow2", [DF, D], F32, **EI)
    dob2 = nc.dram_tensor("ob2", [D], F32, **EI)
    dpfx = nc.dram_tensor("prefix_mask", [8], F32, **EI)
    dout = nc.dram_tensor("out", [T, D], F32, kind="ExternalOutput")

    dts = nc.dram_tensor("tscratch", [cfg["nchunk"], 128, (D // 128) * cfg["chunk"]], F32)
    dX2 = nc.dram_tensor("X2scratch", [T, D], F32)
    dt2tm = nc.dram_tensor("t2tm", [T, D], BF16)
    dt2cm = nc.dram_tensor("t2cm", [cfg["nchunk"], 128, (D // 128) * cfg["chunk"]], BF16)
    deid = nc.dram_tensor("eidscr", [T], F32)
    dstat = nc.dram_tensor("statscr", [2, T], F32)
    dcum = nc.dram_tensor("cumscr", [E, T], F32)
    ddf = nc.dram_tensor("destf", [T], F32)
    divm = nc.dram_tensor("ivmscr", [T], F32)
    dslot = nc.dram_tensor("slotbuf", [E * CAP, D], BF16)
    dmybuf = nc.dram_tensor("mybuf", [CAP, D], BF16)
    dytm = nc.dram_tensor("ytm", [CAP, D], F32)
    dyall = nc.dram_tensor("yall", [E * CAP, D], F32)
    dcnt = nc.dram_tensor("cnt", [E], F32)
    dzero = nc.dram_tensor("dzero", [128 * cfg["chunk"] * min(
        cfg["DF"] // 128, max(1, 1024 // cfg["chunk"]))], BF16)
    dow1b = nc.dram_tensor("ow1b", [D, DF], BF16)
    dow2b = nc.dram_tensor("ow2b", [DF, D], BF16)
    dw1b = nc.dram_tensor("w1b", [D, DF], BF16)
    dw2b = nc.dram_tensor("w2b", [DF, D], BF16)
    dcall = nc.dram_tensor("cntall", [n_cores * E], F32)

    dbg = {}
    if DBG:
        for nm, shp in [("dbg_t", [D, T]), ("dbg_xl", [D, T]),
                        ("dbg_tfin", [D, T]), ("dbg_t2", [T, D]),
                        ("dbg_eid", [T]), ("dbg_dest", [T]), ("dbg_rank", [T]),
                        ("dbg_buf", [CAP, D]), ("dbg_y", [CAP, D]),
                        ("dbg_cnt", [E]), ("dbg_call", [n_cores * E]),
                        ("dbg_base", [8, 2]), ("dbg_pfx", [8, 1]), ("dbg_pbc", [8, 16]), ("dbg_brow", [8, 16]), ("dbg_ivm", [T]), ("dbg_t2cm", [D, T]), ("dbg_uov", [DF, CH]), ("dbg_z", [D, T]), ("dbg_csb", [8, E])]:
            dbg[nm] = nc.dram_tensor(nm, shp, F32, kind="ExternalOutput")

    from contextlib import ExitStack
    stack = ExitStack()

    def sb(name, shape, dtype=F32):
        return stack.enter_context(nc.sbuf_tensor(name, shape, dtype))

    def ps(name, shape, dtype=F32):
        return stack.enter_context(nc.psum_tensor(name, shape, dtype))

    ident_f = sb("ident_f", [128, 128])
    ident_b = sb("ident_b", [128, 128], BF16)
    cst = sb("cst", [128, 10])
    CVALS = [LN_EPS, WKV_EPS, -0.5, -1.0, 1.0, 0.0, float(CAP), BIG, 0.5,
             float(math.log(0.5))]
    LN_HALF = float(math.log(0.5))

    def C(v):
        return cst[:, CVALS.index(v):CVALS.index(v) + 1]

    def C1(v):
        return cst[0:1, CVALS.index(v):CVALS.index(v) + 1]

    ones_row = sb("ones_row", [1, 128])
    ones_col = sb("ones_col", [128, 1])
    cdecay = sb("cdecay", [128, NCT])
    ctf = sb("ctf", [128, NCT])
    cg1 = sb("cg1", [128, NCT])
    cb1 = sb("cb1", [128, NCT])
    cg2 = sb("cg2", [128, NCT])
    cb2 = sb("cb2", [128, NCT])
    cob2 = sb("cob2", [128, NCT])
    cb2e = sb("cb2e", [128, NCT])
    iota8 = sb("iota8", [128, 1])
    cb1e = sb("cb1e", [128, NFT])
    cob1 = sb("cob1", [128, NFT])
    rb_b = sb("rb_b", [128, E])
    pfx_sb = sb("pfx_sb", [8, 1])
    call_sb = sb("call_sb", [8, E])
    base8 = sb("base8", [128, 2])
    raw_cols = sb("raw_cols", [128, 2 * NCT])
    carry_n = sb("carry_n", [128, NCT])
    carry_d = sb("carry_d", [128, NCT])
    carry8 = sb("carry8", [8, 1])
    stat = sb("stat", [128, 16])
    eid_tiles = sb("eid_tiles", [128, NTT])
    desti_t = sb("desti_t", [128, NTT], I32)
    idx_u = sb("idx_u", [128, 8], mybir.dt.uint32)


    AW1 = sb("AW1", [128, NCT, max(DF, 4 * D)], BF16)
    AW2 = sb("AW2", [128, NFT, D], BF16)
    rw_sb = sb("rw_sb", [128, NCT, E])
    A_xtm = [sb(f"A_xtm{i}", [128, TTC * D]) for i in range(2)]
    A_t0 = sb("A_t0", [128, NCT, CH])
    A_xlf_base = sb("A_xlf", [128, NCT * CH])
    A_xlf = A_xlf_base.rearrange("p (a b) -> p a b", a=NCT)
    A_xlb = sb("A_xlb", [128, NCT, CH], BF16)
    A_rwb = sb("A_rwb", [128, NCT, CH], BF16)
    A_dump = sb("A_dump", [128, 2048])

    scr_rows = A_dump[0:1, 2048 - 2 * CH:2048]
    rows = A_dump[0:1, 128:128 + 3 * CH + 32]
    A_u2 = sb("A_u2", [128, NFT, CH], BF16)
    A_ot3 = sb("A_ot3", [128, 12, CH])
    A_otr = [A_ot3[:, i, :] for i in range(12)]
    oh_sb = A_otr[0]
    cum_sb = A_otr[1]
    bcol_kvr = sb("bcol_kvr", [128, 3 * 16])
    bcol2 = sb("bcol2", [128, 2 * NFT])
    repb = sb("repb", [128, NCT, 16], BF16)

    xlb_flat = A_xlb[:, :, :].rearrange("p a b -> p (a b)")
    u2_flat = A_u2[:, :, :].rearrange("p a b -> p (a b)")
    rwb_flat = A_rwb[:, :, :].rearrange("p a b -> p (a b)")
    xlf_flat = A_xlf_base

    pT = [ps(f"pT{i}", [128, 512]) for i in range(2)]
    pB = [ps(f"pB{i}", [128, 512]) for i in range(2)]
    pM = [ps(f"pM{i}", [128, 512]) for i in range(4)]

    for s in ["e_pe", "e_dve", "e_act", "e_pool", "c_cnt", "c_rs", "c_yag"]:
        P.cnt.setdefault(s, 0)

    bank_free = {i: None for i in range(4)}
    rbnds = [None]

    def _setup_rbnds(e):
        rbnds[0] = e.alloc_register("bnds")
        return e.reg_mov(rbnds[0], E * CAP - 1)
    pB_free = [None, None]
    pT_free = [None, None]

    P.do("pool", _setup_rbnds)
    P.do("pool", lambda e: e.iota(ident_f[:, :], pattern=[[1, 128]], base=0,
                                  channel_multiplier=-1,
                                  allow_small_or_imprecise_dtypes=True))
    tok_id0 = P.do("pool", lambda e: e.tensor_single_scalar(
        ident_f[:, :], ident_f[:, :], 0.0, OP.is_equal))
    P.wait("dve", tok_id0)
    tok_idb = P.do("dve", lambda e: e.tensor_copy(ident_b[:, :], ident_f[:, :]))
    P.do("pool", lambda e: e.iota(iota8[:, :], pattern=[[1, 1]], base=0,
                                  channel_multiplier=1,
                                  allow_small_or_imprecise_dtypes=True))
    for _i, _v in enumerate(CVALS):
        P.do("pool", lambda e, i=_i, v=_v: e.memset(cst[:, i:i + 1], v))
    P.do("pool", lambda e: e.memset(ones_row[:, :], 1.0))
    P.do("pool", lambda e: e.memset(ones_col[:, :], 1.0))
    P.do("pool", lambda e: e.memset(carry_n[:, :], 0.0))
    P.do("pool", lambda e: e.memset(carry_d[:, :], 0.0))
    tok_const = P.do("pool", lambda e: e.memset(carry8[:, :], 0.0))

    tw = []
    for j, dsrc in enumerate([dWkT, dWvT, dWrT, dWoT]):
        tw.append(P.dma("pool", AW1[:, :, j * D:(j + 1) * D],
                        dsrc.ap().rearrange("(c p) f -> p c f", p=128)))
    tok_ffv = P.dma("pool", AW2[:, :, :],
                    dWffvT.ap().rearrange("(h p) f -> p h f", p=128))
    tok_cv = [P.dma("pool", dst.ap(), srcd.ap())
              for dst, srcd in [(dow1b, dow1), (dow2b, dow2)]]
    nzt = min(NFT, max(1, D // CH))
    zcols = nzt * CH
    tok_z0 = P.do("pool", lambda e: e.memset(A_u2[:, 0:nzt, :], 0.0))
    zflat = A_u2[:, 0:nzt, :].rearrange("p a b -> p (a b)")
    slot_flat = dslot.ap().rearrange("a b -> (a b)")
    total_el = E * CAP * D
    per_dma = 128 * zcols
    P.wait("pool", tok_z0)
    tok_dz0 = P.dma("pool", dzero.ap().rearrange("(p q) -> p q", q=zcols),
                    zflat[0:128, 0:zcols])
    P.wait("act", tok_dz0)
    P.wait("sp", tok_dz0)
    n_zdma = (total_el + per_dma - 1) // per_dma
    zfill_done = []
    zfill_state = [0]

    def emit_zero_copies(k):
        if cfg.get("skip_zero"):
            return
        i0 = zfill_state[0]
        for i in range(i0, min(i0 + k, n_zdma)):
            e0 = i * per_dma
            nel = min(total_el - e0, per_dma)
            zfill_done.append(P.dma("act" if i % 2 else "sp",
                                    slot_flat[e0:e0 + nel], dzero.ap()[0:nel]))
        zfill_state[0] = min(i0 + k, n_zdma)
    tok_rwl = P.dma("sp", rw_sb[:, :, :],
                    drw.ap().rearrange("(c p) e -> p c e", p=128))
    tok_pfx = P.dma("sp", pfx_sb[:, :], dpfx.ap().rearrange("(a b) -> a b", b=1))

    vecsD = [(dtd, raw_cols[:, 0:NCT], NCT), (dtf, raw_cols[:, NCT:2 * NCT], NCT),
             (dg1, cg1[:, :], NCT), (db1, cb1[:, :], NCT), (dg2, cg2[:, :], NCT),
             (db2, cb2[:, :], NCT), (dob2, cob2[:, :], NCT), (db2e, cb2e[:, :], NCT),
             (db1e, cb1e[:, :], NFT), (dob1, cob1[:, :], NFT)]
    def vband(i, nn, w=128):
        r0 = (i // 8) * 32
        c0 = (i % 8) * 128
        return A_dump[r0:r0 + nn, c0:c0 + w]

    toks_vr = []
    vplaces = []
    i_rb = 0
    toks_vr.append(P.dma("sp", vband(i_rb, 1, E),
                         drb.ap().rearrange("(a b) -> a b", a=1)))
    for i0, (dv, dst, nn) in enumerate(vecsD):
        i = i0 + 1
        toks_vr.append(P.dma("sp", vband(i, nn),
                             dv.ap().rearrange("(c p) -> c p", p=128)))
        vplaces.append((i, nn, dst))

    P.wait_all("pe", toks_vr + [tok_id0, tok_const])
    pcol = 0
    tcols = []
    for (i, nn, dst) in vplaces:
        r0 = (i // 8) * 32
        P.do("pe", lambda e, i=i, nn=nn, pcol=pcol, r0=r0: e.transpose(
            pB[0][0:128, pcol:pcol + nn], vband(i, nn),
            ident_f[r0:r0 + nn, r0:r0 + nn]))
        tcols.append((pcol, nn, dst))
        pcol += nn
    tok_rbb = P.do("pe", lambda e: e.matmul(
        pB[1][:, 0:E], ones_row[0:1, :], vband(i_rb, 1, E),
        start=True, stop=True))
    P.wait("dve", ("e_pe", P.cnt["e_pe"]))
    for (pcol, nn, dst) in tcols:
        P.do("dve", lambda e, pcol=pcol, nn=nn, d=dst: e.tensor_copy(
            d, pB[0][:, pcol:pcol + nn]))
    tok_cc = P.do("dve", lambda e: e.tensor_copy(rb_b[:, :], pB[1][:, 0:E]))
    pB_free = [tok_cc, tok_cc]

    P.wait("act", tok_cc)
    P.wait("act", tok_const)
    P.do("act", lambda e: e.activation(cdecay[:, :], raw_cols[:, 0:NCT], AF.Exp))
    P.do("act", lambda e: e.activation(cdecay[:, :], cdecay[:, :], AF.Exp,
                                       scale=C(-1.0)))
    tok_dec = P.do("act", lambda e: e.activation(ctf[:, :], raw_cols[:, NCT:2 * NCT],
                                                 AF.Exp))
    P.wait("dve", tok_dec)
    P.wait("dve", tok_const)

    f_const_light = ("e_pe", P.cnt["e_pe"])
    P.wait_all("pe", tw[:3] + [tok_idb, tok_const])

    tok_repb = None
    for ct in range(NCT):
        tok_repb = P.do("dve", lambda e, ct=ct: e.tensor_scalar_mul(
            repb[:, ct, :], C(1.0).broadcast_to([128, 16]), cb1[:, ct:ct + 1]))
    P.wait("pe", tok_repb)
    for p_ in range(3):
        tok_bm = None
        for o in range(NCT):
            for ct in range(NCT):
                tok_bm = P.do("pe", lambda e, p_=p_, o=o, ct=ct: e.matmul(
                    pM[p_][:, o * 16:(o + 1) * 16],
                    AW1[:, ct, p_ * D + o * 128:p_ * D + (o + 1) * 128],
                    repb[:, ct, :], start=(ct == 0), stop=(ct == NCT - 1)))
        P.wait("dve", tok_bm)
        bview = bcol_kvr[:, p_ * 16:(p_ + 1) * 16].rearrange("p (a b) -> p a b", b=2)
        pview = pM[p_][:, 0:128].rearrange("p (a b) -> p a b", b=16)[:, :, 0:2]
        if p_ == 2:
            tok_e = P.do("dve", lambda e, bview=bview, pview=pview:
                         e.tensor_scalar_mul(bview, pview, -1.0))
        else:
            tok_e = P.do("dve", lambda e, bview=bview, pview=pview:
                         e.tensor_copy(bview, pview))
        bank_free[p_] = tok_e
    f_bias_pe = ("e_pe", P.cnt["e_pe"])
    P.wait("dve", f_bias_pe)
    tok_wfold = None
    for ct in range(NCT):
        tok_wfold = P.do("dve", lambda e, ct=ct: e.tensor_scalar_mul(
            AW1[:, ct, 0:3 * D], AW1[:, ct, 0:3 * D], cg1[:, ct:ct + 1]))
    wfold = [tok_wfold]

    f_const_pe = ("e_pe", P.cnt["e_pe"])


    def ln_stats_tm(xt, with_affine, pre_tok=None, dump=None):
        if dump is None:
            dump = A_dump
        if pre_tok is not None:
            P.wait("dve", pre_tok)
            P.wait("act", pre_tok)
        tok_sq = None
        for tt in range(TTC):
            P.do("dve", lambda e, tt=tt: e.reduce_sum(
                stat[:, tt:tt + 1], xt[:, tt * D:(tt + 1) * D], axis=AX.X),
                nochain=(tt > 0))
            tok_sq = P.do("act", lambda e, tt=tt, dump=dump: e.activation(
                dump[:, tt * D:(tt + 1) * D], xt[:, tt * D:(tt + 1) * D],
                AF.Square, accum_out=stat[:, 4 + tt:5 + tt]), nochain=(tt > 0))
        P.wait("dve", tok_sq)
        P.do("dve", lambda e: e.tensor_scalar_mul(stat[:, 8:8 + TTC],
                                                  stat[:, 0:TTC], inv_d))
        P.do("dve", lambda e: e.tensor_tensor(stat[:, 12:12 + TTC],
                                              stat[:, 8:8 + TTC],
                                              stat[:, 8:8 + TTC], OP.mult))
        tok_var = P.do("dve", lambda e: e.scalar_tensor_tensor(
            stat[:, 4:4 + TTC], stat[:, 4:4 + TTC], inv_d, stat[:, 12:12 + TTC],
            OP.mult, OP.subtract))
        P.wait("act", tok_var)
        P.do("act", lambda e: e.activation(stat[:, 12:12 + TTC], stat[:, 4:4 + TTC],
                                           AF.Ln, bias=C(LN_EPS)))
        tok_s0 = P.do("act", lambda e: e.activation(
            stat[:, 12:12 + TTC], stat[:, 12:12 + TTC], AF.Exp, scale=C(-0.5)))
        if not with_affine:
            return tok_s0
        P.wait("dve", tok_s0)
        P.do("dve", lambda e: e.tensor_tensor(stat[:, 0:TTC], stat[:, 12:12 + TTC],
                                              stat[:, 12:12 + TTC], OP.mult))
        tok_v1 = P.do("dve", lambda e: e.tensor_tensor(
            stat[:, 0:TTC], stat[:, 0:TTC], stat[:, 4:4 + TTC], OP.mult))
        P.wait("act", tok_v1)
        P.do("act", lambda e: e.activation(stat[:, 0:TTC], stat[:, 0:TTC],
                                           AF.Ln, bias=C(LN_EPS)))
        tok_s1 = P.do("act", lambda e: e.activation(stat[:, 0:TTC], stat[:, 0:TTC],
                                                    AF.Exp, scale=C(-0.5)))
        return tok_s1

    def bcast_cols(cols, tok_ready):
        P.wait("pe", tok_ready)
        P.wait_all("pe", pB_free)
        tok_tpr = None
        for (col, bi) in cols:
            for tt in range(TTC):
                tok_tpr = P.do("pe", lambda e, col=col, bi=bi, tt=tt: e.transpose(
                    pB[bi][0:1, tt * 128:(tt + 1) * 128],
                    stat[:, col + tt:col + tt + 1], ident_f[:, :]))
        P.wait("act", tok_tpr)
        tok_rows = None
        for i, (col, bi) in enumerate(cols):
            tok_rows = P.do("act", lambda e, bi=bi, i=i: e.copy(
                scr_rows[0:1, i * CH:(i + 1) * CH], pB[bi][0:1, 0:CH]))
        P.wait("pe", tok_rows)
        tok_b = []
        for i, (col, bi) in enumerate(cols):
            tok_b.append(P.do("pe", lambda e, bi=bi, i=i: e.matmul(
                pB[bi][:, 0:CH], ones_row[0:1, :],
                scr_rows[0:1, i * CH:(i + 1) * CH], start=True, stop=True)))
        return tok_b

    def mm_group(bank_idx, lhsT_fn, rhs_fn, K, consume, N=None):
        N = CH if N is None else N
        bk = pM[bank_idx]
        P.wait("pe", bank_free[bank_idx])
        tok_mm = None
        for k in range(K):
            tok_mm = P.do("pe", lambda e, k=k, bk=bk, N=N: e.matmul(
                bk[:, 0:N], lhsT_fn(k), rhs_fn(k),
                start=(k == 0), stop=(k == K - 1)))
        tok_e = consume(bk, tok_mm)
        bank_free[bank_idx] = tok_e
        return tok_e

    def cm_to_tm(src3, dst_flat, tok_src, evac_eng="act"):
        P.wait("pe", tok_src)
        tok_tm = None
        for tt in range(TTC):
            for half in range(2):
                bi = half
                P.wait("pe", pT_free[bi])
                tok_tp = None
                for j in range(4):
                    ct = half * 4 + j
                    tok_tp = P.do("pe", lambda e, ct=ct, tt=tt, bi=bi, j=j: e.transpose(
                        pT[bi][0:128, j * 128:(j + 1) * 128],
                        src3[:, ct, tt * 128:(tt + 1) * 128], ident_f[:, :]))
                P.wait(evac_eng, tok_tp)
                tok_tm = P.do(evac_eng, lambda e, tt=tt, half=half, bi=bi: e.copy(
                    dst_flat[:, tt * D + half * 512: tt * D + (half + 1) * 512],
                    pT[bi][0:128, 0:512]), nochain=(tt + half > 0))
                pT_free[bi] = tok_tm
        return tok_tm

    chunk_f = [[], []]
    otr_free = {}

    p1_prev = {}

    def p1_front_a(c):
        pr = c % 2
        xt = A_xtm[pr]
        if c >= 2:
            P.wait_all("sp", chunk_f[(c - 2) % 2])
        tok_x = P.dma("sp", xt.rearrange("p (tt f) -> p tt f", tt=TTC),
                      dx.ap()[c * CH:(c + 1) * CH, :]
                      .rearrange("(tt p) f -> p tt f", p=128))
        if c >= 1:
            P.wait("dve", p1_prev["pe_after_b"])
            P.wait("act", p1_prev["pe_after_b"])
            P.wait("act", p1_prev.get("ln2_qs_pe"))
        else:
            P.wait("act", f_const_light)
            P.wait("dve", f_const_light)
        tok_s1 = ln_stats_tm(xt, True, pre_tok=tok_x, dump=u2_flat)
        tok_u0 = None
        for tt in range(TTC):
            tok_u0 = P.do("dve", lambda e, tt=tt: e.tensor_scalar_sub(
                xt[:, tt * D:(tt + 1) * D], xt[:, tt * D:(tt + 1) * D],
                stat[:, 8 + tt:9 + tt]), nochain=(tt > 0))
        p1_prev["s1"] = tok_s1
        p1_prev["u0"] = tok_u0

    t0bufs = [A_t0, A_xlf]

    def p1_front_b(c):
        pr = c % 2
        xt = A_xtm[pr]
        t0b = t0bufs[pr]
        P.wait_all("act", p1_prev.get("srows_st") or [])
        tok_b = bcast_cols([(12, 0), (0, 1)], p1_prev["s1"])
        p1_prev["pe_after_b"] = ("e_pe", P.cnt["e_pe"])
        P.wait("pe", p1_prev["u0"])
        P.wait_all("act", p1_prev.get(("t0f", pr)) or [])
        tbanks = [pT[0], pT[1], pM[2], pM[3]]
        tfree = [pT_free[0], pT_free[1], bank_free[2], bank_free[3]]
        P.wait_all("dve", (p1_prev.get(("t0f", pr)) or [])
                   + [tok_b[0], tok_b[1], p1_prev.get("pe_after_kvr")])
        s0bc = A_dump[:, 0:CH]
        P.do("dve", lambda e, s0bc=s0bc: e.tensor_copy(s0bc, pB[0][:, 0:CH]))
        for ct in range(NCT):
            bi = ct % 4
            bk = tbanks[bi]
            P.wait("pe", tfree[bi])
            tok_tr = None
            for tt in range(TTC):
                tok_tr = P.do("pe", lambda e, ct=ct, tt=tt, bk=bk: e.transpose(
                    bk[0:128, tt * 128:(tt + 1) * 128],
                    xt[:, tt * D + ct * 128: tt * D + (ct + 1) * 128],
                    ident_f[:, :]))
            P.wait("dve", tok_tr)
            tok_ev = P.do("dve", lambda e, ct=ct, bk=bk, t0b=t0b, s0bc=s0bc:
                          e.tensor_tensor(
                              t0b[:, ct, :], bk[0:128, 0:CH], s0bc, OP.mult),
                          nochain=(ct > 0))
            tfree[bi] = tok_ev
        pT_free[0], pT_free[1] = tfree[0], tfree[1]
        bank_free[2], bank_free[3] = tfree[2], tfree[3]
        rep1 = pB[1][:, 0:CH].rearrange("p (a b) -> p a b", a=1) \
            .broadcast_to([128, NCT, CH])
        tok_xl = P.do("dve", lambda e, t0b=t0b, rep1=rep1: e.tensor_tensor(
            A_xlb[:, :, :], t0b[:, :, :], rep1, OP.mult))
        pB_free[0] = tok_xl
        pB_free[1] = tok_xl
        p1_prev["xl"] = tok_xl
        if DBG:
            P.wait("pool", tok_xl)
            tok_d = P.dma("pool", dbg["dbg_xl"].ap()
                          .rearrange("(ct p) t -> p ct t", p=128)
                          [:, :, c * CH:(c + 1) * CH],
                          A_xlb[:, :, :])
            P.wait("dve", tok_d)

    def p1_back(c):
        P.wait("pe", p1_prev["xl"])
        if c == 0:
            P.wait("pe", wfold[0])
        GO = 4
        weG = A_ot3[:, 0:GO, :]
        vvG = A_ot3[:, GO:2 * GO, :]
        sgG = A_ot3[:, 2 * GO:3 * GO, :]
        tok_rw_last = None
        for og in range(0, NCT, GO):
            os_ = list(range(og, og + GO))
            weS = {o: A_otr[o % GO] for o in os_}
            vvS = {o: A_otr[4 + o % GO] for o in os_}
            sgS = {o: A_otr[8 + o % GO] for o in os_}
            war = otr_free.get("p1")
            toks_e = {}
            for o in os_:
                def ev_k(bk, tok_mm, war=war, we=weS[o], o=o):
                    P.wait("act", tok_mm)
                    P.wait("act", war)
                    return P.do("act", lambda e, bk=bk, we=we, o=o: e.activation(
                        we[:, :], bk[:, 0:CH], AF.Exp,
                        bias=bcol_kvr[:, 2 * o:2 * o + 1]), nochain=(o % 4 > 0))
                toks_e[(o, 0)] = mm_group(o % 4,
                                          lambda k, o=o: AW1[:, k, o * 128:(o + 1) * 128],
                                          lambda k: A_xlb[:, k, :], NCT, ev_k)
            if og == 0 and c + 1 < NCH:
                p1_front_a(c + 1)
            for o in os_:
                def ev_v(bk, tok_mm, vv=vvS[o], o=o):
                    P.wait("act", tok_mm)
                    return P.do("act", lambda e, bk=bk, vv=vv, o=o: e.activation(
                        vv[:, :], bk[:, 0:CH], AF.Identity,
                        bias=bcol_kvr[:, 16 + 2 * o:16 + 2 * o + 1]),
                        nochain=True)
                toks_e[(o, 1)] = mm_group(o % 4,
                                          lambda k, o=o: AW1[:, k, D + o * 128:D + (o + 1) * 128],
                                          lambda k: A_xlb[:, k, :], NCT, ev_v)
            for o in os_:
                def ev_r(bk, tok_mm, sg=sgS[o], o=o):
                    P.wait("act", tok_mm)
                    return P.do("act", lambda e, bk=bk, sg=sg, o=o: e.activation(
                        sg[:, :], bk[:, 0:CH], AF.Exp, scale=C(-1.0),
                        bias=bcol_kvr[:, 32 + 2 * o:32 + 2 * o + 1]),
                        nochain=True)
                toks_e[(o, 2)] = mm_group(o % 4,
                                          lambda k, o=o: AW1[:, k, 2 * D + o * 128:2 * D + (o + 1) * 128],
                                          lambda k: A_xlb[:, k, :], NCT, ev_r)
            P.wait_all("dve", [toks_e[(o, 0)] for o in os_]
                       + [toks_e[(o, 1)] for o in os_]
                       + [toks_e[(o, 2)] for o in os_])
            P.do("dve", lambda e: e.scalar_tensor_tensor(
                vvG, weG, EXP_CLAMP, vvG, OP.min, OP.mult))
            P.do("dve", lambda e: e.tensor_scalar_min(weG, weG, EXP_CLAMP))
            P.do("dve", lambda e: e.tensor_scalar_add(sgG, sgG, 1.0),
                 nochain=True)
            if c == 0:
                ctfv = ctf[:, og:og + GO].rearrange("p (a b) -> p a b", b=1)
                P.do("dve", lambda e, ctfv=ctfv: e.tensor_tensor(
                    vvG[:, :, 0:1], vvG[:, :, 0:1], ctfv, OP.mult))
                P.do("dve", lambda e, ctfv=ctfv: e.tensor_tensor(
                    weG[:, :, 0:1], weG[:, :, 0:1], ctfv, OP.mult))
            if c == 0 and og == 0:
                P.wait("dve", tok_dec)
            first_scan = True
            for o in os_:
                we, vv = weS[o], vvS[o]
                P.do("dve", lambda e, vv=vv, o=o, c=c: e.tensor_tensor_scan(
                    vv[:, :], cdecay[:, o:o + 1].broadcast_to([128, CH]), vv[:, :],
                    0.0 if c == 0 else carry_n[:, o:o + 1], OP.mult, OP.add),
                    nochain=not first_scan)
                first_scan = False
                P.do("dve", lambda e, we=we, o=o, c=c: e.tensor_tensor_scan(
                    we[:, :], cdecay[:, o:o + 1].broadcast_to([128, CH]), we[:, :],
                    0.0 if c == 0 else carry_d[:, o:o + 1], OP.mult, OP.add),
                    nochain=True)
            P.do("dve", lambda e, og=og: e.tensor_copy(
                carry_n[:, og:og + GO].rearrange("p (a b) -> p a b", b=1),
                vvG[:, :, CH - 1:CH]))
            tok_scan = P.do("dve", lambda e, og=og: e.tensor_copy(
                carry_d[:, og:og + GO].rearrange("p (a b) -> p a b", b=1),
                weG[:, :, CH - 1:CH]), nochain=True)
            tok_d2 = P.do("dve", lambda e: e.scalar_tensor_tensor(
                weG, weG, WKV_EPS, sgG, OP.add, OP.mult))
            P.wait("act", tok_d2)
            P.do("act", lambda e: e.activation(weG, weG, AF.Ln))
            tok_rd = P.do("act", lambda e: e.activation(
                weG, weG, AF.Exp, scale=C(-1.0)))
            P.wait("dve", tok_rd)
            tok_rw_last = P.do("dve", lambda e, og=og: e.tensor_tensor(
                A_rwb[:, og:og + GO, :], vvG, weG, OP.mult))
            otr_free["p1"] = tok_rw_last
        p1_prev["pe_after_kvr"] = ("e_pe", P.cnt["e_pe"])
        p1_prev["rw_last"] = tok_rw_last

    def p1_back_wo(c):
        tok_rw_last = p1_prev["rw_last"]
        t0b = t0bufs[c % 2]
        P.wait("pe", tok_rw_last)
        if c == 0:
            P.wait("pe", tw[3])
        tok_t = None
        for o in range(NCT):
            def ev_t(bk, tok_mm, o=o, t0b=t0b):
                P.wait("dve", tok_mm)
                return P.do("dve", lambda e, bk=bk, o=o, t0b=t0b: e.tensor_tensor(
                    t0b[:, o, :], t0b[:, o, :], bk[:, 0:CH], OP.add),
                    nochain=(o > 0))
            tok_t = mm_group(o % 4,
                             lambda k, o=o: AW1[:, k, 3 * D + o * 128:3 * D + (o + 1) * 128],
                             lambda k: A_rwb[:, k, :], NCT, ev_t)
        p1_prev["tok_t"] = tok_t

    def p1_back_tail(c):
        t0b = t0bufs[c % 2]
        tok_t = p1_prev["tok_t"]
        P.wait("act", tok_t)
        tok_tst = P.dma("act", dts.ap()[c].rearrange("p (a b) -> p a b", a=NCT),
                        t0b[:, :, :])
        if DBG:
            P.wait("pool", tok_t)
            tok_d = P.dma("pool", dbg["dbg_t"].ap()
                          .rearrange("(ct p) t -> p ct t", p=128)
                          [:, :, c * CH:(c + 1) * CH], t0b[:, :, :])
            P.wait("dve", tok_d)

        P.wait_all("pe", [tok_t, pB_free[0], pB_free[1]])
        tok_ms = None
        for ct in range(NCT):
            tok_ms = P.do("pe", lambda e, ct=ct, t0b=t0b: e.matmul(
                pB[0][0:1, 0:CH], ones_col[:, 0:1], t0b[:, ct, :],
                start=(ct == 0), stop=(ct == NCT - 1)))
        P.wait("act", tok_t)
        P.wait("act", otr_free.get("p1"))
        tok_sq = P.do("act", lambda e, t0b=t0b: e.activation(
            A_ot3[:, 4:4 + NCT, :], t0b[:, :, :], AF.Square))
        P.wait("pe", tok_sq)
        tok_qs = None
        for ct in range(NCT):
            tok_qs = P.do("pe", lambda e, ct=ct: e.matmul(
                pB[1][0:1, 0:CH], ones_col[:, 0:1], A_ot3[:, 4 + ct, :],
                start=(ct == 0), stop=(ct == NCT - 1)))
        p1_prev["ln2_qs_pe"] = ("e_pe", P.cnt["e_pe"])
        otr_free["p1"] = p1_prev["ln2_qs_pe"]
        p1_prev[("t0f", c % 2)] = [tok_tst, p1_prev["ln2_qs_pe"], tok_sq]
        P.wait_all("dve", [tok_ms, tok_qs])
        P.wait_all("dve", p1_prev.get("srows_st") or [])
        P.do("dve", lambda e: e.tensor_scalar_mul(scr_rows[0:1, 0:CH],
                                                  pB[0][0:1, 0:CH], inv_d))
        P.do("dve", lambda e: e.tensor_tensor(scr_rows[0:1, CH:2 * CH],
                                              scr_rows[0:1, 0:CH],
                                              scr_rows[0:1, 0:CH], OP.mult))
        tok_vr2 = P.do("dve", lambda e: e.scalar_tensor_tensor(
            scr_rows[0:1, CH:2 * CH], pB[1][0:1, 0:CH], inv_d,
            scr_rows[0:1, CH:2 * CH], OP.mult, OP.subtract))
        pB_free[0] = tok_vr2
        pB_free[1] = tok_vr2
        P.wait("act", tok_vr2)
        P.do("act", lambda e: e.activation(scr_rows[0:1, CH:2 * CH],
                                           scr_rows[0:1, CH:2 * CH],
                                           AF.Ln, bias=C1(LN_EPS)))
        tok_s2 = P.do("act", lambda e: e.activation(
            scr_rows[0:1, CH:2 * CH], scr_rows[0:1, CH:2 * CH],
            AF.Exp, scale=C1(-0.5)))
        P.wait("sp", tok_s2)
        t1 = P.dma("sp", dstat.ap()[0:1, c * CH:(c + 1) * CH],
                   scr_rows[0:1, 0:CH])
        t2_ = P.dma("sp", dstat.ap()[1:2, c * CH:(c + 1) * CH],
                    scr_rows[0:1, CH:2 * CH])
        p1_prev["srows_st"] = [t1, t2_]

    if not cfg.get("skip_p1"):
        p1_front_a(0)
        p1_front_b(0)
        for c in range(NCH):
            p1_back(c)
            if c + 1 < NCH:
                p1_front_b(c + 1)
            p1_back_wo(c)
            p1_back_tail(c)
            emit_zero_copies((n_zdma + NCH - 1) // NCH)
            chunk_f[c % 2] = P.fence_all()
        emit_zero_copies(n_zdma)
    else:
        p1_front_a(0)
        p1_front_b(0)
        p1_back(0)
        p1_back_wo(0)
        p1_back_tail(0)
        chunk_f[0] = P.fence_all()

    f_p1 = P.fence_all()
    P.wait_all("pool", f_p1)
    P.wait_all("sp", f_p1)
    P.wait_all("dve", f_p1)
    P.wait_all("act", f_p1)
    P.wait_all("pe", f_p1)
    tok_ffk = P.dma("pool", AW1[:, :, 0:DF],
                    dWffkT.ap().rearrange("(c p) f -> p c f", p=128))
    tok_cv_w = [P.dma("pool", dst.ap(), srcd.ap())
                for dst, srcd in [(dw1b, dw1), (dw2b, dw2)]]

    tok_repb2 = None
    for ct in range(NCT):
        tok_repb2 = P.do("dve", lambda e, ct=ct: e.tensor_scalar_mul(
            repb[:, ct, :], C(1.0).broadcast_to([128, 16]), cb2[:, ct:ct + 1]))
    P.wait_all("pe", [tok_ffk, tok_ffv, tok_repb2])
    tok_bm2 = None
    for h in range(NFT):
        for ct in range(NCT):
            tok_bm2 = P.do("pe", lambda e, h=h, ct=ct: e.matmul(
                pM[0][:, h * 16:(h + 1) * 16],
                AW1[:, ct, h * 128:(h + 1) * 128],
                repb[:, ct, :], start=(ct == 0), stop=(ct == NCT - 1)))
    P.wait("dve", tok_bm2)
    tok_bc2 = P.do("dve", lambda e: e.tensor_copy(
        bcol2[:, :].rearrange("p (a b) -> p a b", b=2),
        pM[0][:, 0:16 * NFT].rearrange("p (a b) -> p a b", b=16)[:, :, 0:2]))
    bank_free[0] = tok_bc2
    tok_wf2 = None
    for ct in range(NCT):
        tok_wf2 = P.do("dve", lambda e, ct=ct: e.tensor_scalar_mul(
            AW1[:, ct, 0:DF], AW1[:, ct, 0:DF], cg2[:, ct:ct + 1]))
    P.wait("pe", tok_wf2)

    p2s = {}

    def p2_chunk(c):
        pr = c % 2
        xt = A_xtm[pr]
        P.wait("sp", p2s.get("t0_free"))
        if DBG:
            P.wait("sp", p2s.get("dbg_tfin"))
        tok_t = P.dma("sp", A_t0[:, :, :],
                      dts.ap()[c].rearrange("p (a b) -> p a b", a=NCT))
        P.wait("sp", p2s.get("brows_pe"))
        tok_m = P.dma("sp", scr_rows[0:1, 0:CH],
                      dstat.ap()[0:1, c * CH:(c + 1) * CH])
        tok_s2 = P.dma("sp", scr_rows[0:1, CH:2 * CH],
                       dstat.ap()[1:2, c * CH:(c + 1) * CH])
        if c >= 1:
            p2_router(c - 1)
        P.wait_all("pe", [tok_t, tok_m, tok_s2] + pB_free)
        tok_b0 = P.do("pe", lambda e: e.matmul(
            pB[0][:, 0:CH], ones_row[0:1, :], scr_rows[0:1, 0:CH],
            start=True, stop=True))
        tok_b1 = P.do("pe", lambda e: e.matmul(
            pB[1][:, 0:CH], ones_row[0:1, :], scr_rows[0:1, CH:2 * CH],
            start=True, stop=True))
        p2s["brows_pe"] = ("e_pe", P.cnt["e_pe"])
        P.wait_all("dve", [tok_b0, tok_b1, p2s.get("router_pe"),
                           p2s.get("u2_pe"), p2s.get("t2tm_dma")])
        rep0 = pB[0][:, 0:CH].rearrange("p (a b) -> p a b", a=1) \
            .broadcast_to([128, NCT, CH])
        rep1 = pB[1][:, 0:CH].rearrange("p (a b) -> p a b", a=1) \
            .broadcast_to([128, NCT, CH])
        xlb = A_xlb
        xlbf = xlb_flat
        P.do("dve", lambda e, rep0=rep0: e.tensor_tensor(
            A_xlf[:, :, :], A_t0[:, :, :], rep0, OP.subtract))
        tok_xl2 = P.do("dve", lambda e, rep1=rep1, xlb=xlb: e.tensor_tensor(
            xlb[:, :, :], A_xlf[:, :, :], rep1, OP.mult))
        pB_free[0] = tok_xl2
        pB_free[1] = tok_xl2

        P.wait("pe", tok_xl2)
        tok_u2 = None
        for h in range(NFT):
            war = otr_free.get(("p2", h % 2))

            def ev_u(bk, tok_mm, h=h, war=war):
                P.wait("act", tok_mm)
                P.wait("act", war)
                P.wait("act", p2s.get("tfin_pe"))
                tok_rl = P.do("act", lambda e, bk=bk, h=h: e.activation(
                    A_otr[8 + h % 2][:, :], bk[:, 0:CH], AF.Relu,
                    bias=bcol2[:, 2 * h:2 * h + 1]))
                P.wait("dve", tok_rl)
                return P.do("dve", lambda e, h=h: e.tensor_tensor(
                    A_u2[:, h, :], A_otr[8 + h % 2][:, :], A_otr[8 + h % 2][:, :],
                    OP.mult), nochain=(h > 0))
            tok_u2 = mm_group(h % 4,
                              lambda k, h=h: AW1[:, k, h * 128:(h + 1) * 128],
                              lambda k, xlb=xlb: xlb[:, k, :], NCT, ev_u)
            otr_free[("p2", h % 2)] = tok_u2
        p2s["u2_pe"] = ("e_pe", P.cnt["e_pe"])
        P.wait("pe", tok_u2)
        tok_tf = None
        for co in range(NCT):
            def ev_tf(bk, tok_mm, co=co):
                P.wait("dve", tok_mm)
                return P.do("dve", lambda e, bk=bk, co=co: e.tensor_tensor(
                    A_t0[:, co, :], A_t0[:, co, :], bk[:, 0:CH], OP.add),
                    nochain=(co > 0))
            tok_tf = mm_group(co % 4,
                              lambda k, co=co: AW2[:, k, co * 128:(co + 1) * 128],
                              lambda k: A_u2[:, k, :], NFT, ev_tf)
        if DBG:
            P.wait("pool", tok_tf)
            tok_d = P.dma("pool", dbg["dbg_tfin"].ap()
                          .rearrange("(ct p) t -> p ct t", p=128)
                          [:, :, c * CH:(c + 1) * CH], A_t0[:, :, :])
            P.wait("dve", tok_d)
            p2s["dbg_tfin"] = tok_d

        p2s["tfin_pe"] = ("e_pe", P.cnt["e_pe"])
        xo = A_xtm[(c + 1) % 2]
        P.wait("act", p2s.get("x2st"))
        P.wait("act", p2s.get("t2b"))
        tok_tm = cm_to_tm(A_t0, xo, tok_tf)
        p2s["t0_free"] = ("e_pe", P.cnt["e_pe"])

        P.wait("act", p2s.get("x2_dve"))
        tok_X = P.dma("act", xt.rearrange("p (tt f) -> p tt f", tt=TTC),
                      dx.ap()[c * CH:(c + 1) * CH, :]
                      .rearrange("(tt p) f -> p tt f", p=128))
        P.wait_all("dve", [tok_X, tok_tm])
        tok_x2 = None
        for tt in range(TTC):
            tok_x2 = P.do("dve", lambda e, tt=tt: e.tensor_tensor(
                xt[:, tt * D:(tt + 1) * D], xt[:, tt * D:(tt + 1) * D],
                xo[:, tt * D:(tt + 1) * D], OP.add), nochain=(tt > 0))
        p2s["x2_dve"] = tok_x2
        P.wait("act", tok_x2)
        p2s["x2st"] = P.dma("act", dX2.ap()[c * CH:(c + 1) * CH, :]
                            .rearrange("(tt p) f -> p tt f", p=128),
                            xt.rearrange("p (tt f) -> p tt f", tt=TTC))
        tok_s0p = ln_stats_tm(xt, False, pre_tok=tok_x2, dump=u2_flat)
        P.wait("dve", tok_s0p)
        tok_t2b = None
        for tt in range(TTC):
            tok_t2b = P.do("dve", lambda e, tt=tt, xlbf=xlbf: e.tensor_scalar(
                xlbf[:, tt * D:(tt + 1) * D], xt[:, tt * D:(tt + 1) * D],
                stat[:, 8 + tt:9 + tt], stat[:, 12 + tt:13 + tt],
                OP.subtract, OP.mult), nochain=(tt > 0))
        p2s["t2b"] = tok_t2b
        P.wait("pool", tok_t2b)
        p2s["t2tm_dma"] = P.dma(
            "pool", dt2tm.ap()[c * CH:(c + 1) * CH, :]
            .rearrange("(tt p) f -> p tt f", p=128),
            xlbf[:, 0:TTC * D].rearrange("p (tt f) -> p tt f", tt=TTC))
        if DBG:
            tok_d = P.dma("pool", dbg["dbg_t2"].ap()[c * CH:(c + 1) * CH, :]
                          .rearrange("(tt p) f -> p tt f", p=128),
                          xlbf[:, 0:TTC * D].rearrange("p (tt f) -> p tt f", tt=TTC))
            P.wait("dve", tok_d)

    nsl = 1024 // CH
    stg8 = A_ot3[:, 0:nsl, :].rearrange("p a (b c) -> p (a b) c", c=128)

    def p2_router(c):
        P.wait("dve", p2s.get("t2cm_dma"))
        P.wait("pe", p2s.get("t2b"))
        tok_cmb = None
        for tt in range(TTC):
            g = c * TTC + tt
            for half in range(2):
                bi = half
                P.wait("pe", pT_free[bi])
                tok_tp3 = None
                for j in range(4):
                    ct = half * 4 + j
                    tok_tp3 = P.do("pe", lambda e, ct=ct, tt=tt, bi=bi, j=j:
                                   e.matmul(
                                       pT[bi][0:128, j * 128:(j + 1) * 128],
                                       xlb_flat[:, tt * D + ct * 128: tt * D + (ct + 1) * 128],
                                       ident_b[:, :], start=True, stop=True))
                P.wait("act", tok_tp3)
                pview = pT[bi][0:128, 0:512].rearrange("p (a b) -> p a b", b=128)
                stg = A_ot3[:, half * (nsl // 2):(half + 1) * (nsl // 2), :]
                tok_rtl = P.do("act", lambda e, stg=stg, pview=pview: e.copy(
                    stg, pview))
                P.wait("dve", tok_rtl)
                tok_cmb = P.do("dve", lambda e, half=half, tt=tt, pview=pview:
                               e.tensor_copy(
                                   A_rwb[:, half * 4:(half + 1) * 4,
                                         tt * 128:(tt + 1) * 128], pview))
                pT_free[bi] = tok_cmb
                P.wait("pe", tok_cmb)
            P.wait("pe", bank_free[0])
            tok_rmm = None
            for ct in range(NCT):
                tok_rmm = P.do("pe", lambda e, ct=ct: e.matmul(
                    pM[0][:, 0:E], stg8[:, ct, :], rw_sb[:, ct, :],
                    start=(ct == 0), stop=(ct == NCT - 1)))
            P.wait("dve", tok_rmm)
            tok_lg = P.do("dve", lambda e: e.tensor_tensor(
                A_otr[6][:, 0:E], pM[0][:, 0:E], rb_b[:, 0:E], OP.add))
            bank_free[0] = tok_lg
            P.do("dve", lambda e: e.max(A_otr[7][:, 0:8], A_otr[6][:, 0:E]))
            P.do("dve", lambda e: e.max_index(
                idx_u[:, 0:8], A_otr[7][:, 0:8], A_otr[6][:, 0:E]))
            tok_eid = P.do("dve", lambda e, g=g: e.tensor_copy(
                eid_tiles[:, g:g + 1], idx_u[:, 0:1]))
        p2s["router_pe"] = ("e_pe", P.cnt["e_pe"])
        p2s["router_dve"] = tok_eid
        P.wait("sp", tok_cmb)
        p2s["t2cm_dma"] = P.dma(
            "sp", dt2cm.ap()[c].rearrange("p (a b) -> p a b", a=NCT),
            A_rwb[:, :, :])

    NP2 = NCH if not cfg.get("skip_p2") else 1
    for c in range(NP2):
        p2_chunk(c)
    p2_router(NP2 - 1)

    f_p2 = P.fence_all()
    P.wait_all("sp", f_p2)
    P.wait_all("dve", f_p2)
    P.wait_all("act", f_p2)

    P.wait_all("pe", f_p2)
    P.wait_all("pool", f_p2)

    P.wait_all("act", tok_cv)
    P.wait_all("sp", tok_cv)
    tok_ow1 = P.dma("act", AW1[:, :, 0:DF],
                    dow1b.ap().rearrange("(c p) f -> p c f", p=128))
    tok_ow2 = P.dma("sp", AW2[:, :, :],
                    dow2b.ap().rearrange("(h p) f -> p h f", p=128))
    p34 = {}

    def yov_chunk(c, phase="all"):
        if phase in ("all", "uov"):
            P.wait("sp", p34.get("rwb_pe"))
            P.wait("sp", p34.get("u2_pe"))
            tok_t2 = P.dma("sp", A_rwb[:, :, :],
                           dt2cm.ap()[c].rearrange("p (a b) -> p a b", a=NCT))
            P.wait_all("pe", [tok_t2, tok_ow1, p34.get("u2_pe")])
            tok_u = None
            for h in range(NFT):
                def ev_uo(bk, tok_mm, h=h):
                    P.wait("act", tok_mm)
                    P.wait("act", p34.get("u2_pe"))
                    return P.do("act", lambda e, bk=bk, h=h: e.activation(
                        A_u2[:, h, :], bk[:, 0:CH], AF.Relu,
                        bias=cob1[:, h:h + 1]), nochain=(h > 0))
                tok_u = mm_group(h % 4,
                                 lambda k, h=h: AW1[:, k, h * 128:(h + 1) * 128],
                                 lambda k: A_rwb[:, k, :], NCT, ev_uo)
            p34["rwb_pe"] = ("e_pe", P.cnt["e_pe"])
            p34[("mid", c)] = tok_u
            if phase == "uov":
                return
        tok_u = p34.pop(("mid", c))
        tok_iv = P.dma("sp", A_otr[6][0:1, 0:CH],
                       divm.ap().rearrange("(a b) -> a b", a=1)
                       [0:1, c * CH:(c + 1) * CH])
        P.wait_all("pe", [tok_iv] + pB_free)
        tok_ivb = P.do("pe", lambda e: e.matmul(
            pB[0][:, 0:CH], ones_row[0:1, :], A_otr[6][0:1, 0:CH],
            start=True, stop=True))
        P.wait("dve", tok_ivb)
        tok_ivs = P.do("dve", lambda e: e.tensor_copy(A_otr[7][:, :],
                                                      pB[0][:, 0:CH]))
        pB_free[0] = tok_ivs
        P.wait_all("pe", [tok_u, tok_ow2])
        tok_zc = None
        for co in range(NCT):
            def ev_z(bk, tok_mm, co=co):
                P.wait_all("dve", [tok_mm, p34.get("zst")])
                P.do("dve", lambda e, bk=bk, co=co: e.tensor_scalar_add(
                    A_t0[:, co, :], bk[:, 0:CH], cob2[:, co:co + 1]),
                    nochain=(co > 0))
                return P.do("dve", lambda e, co=co: e.tensor_tensor(
                    A_t0[:, co, :], A_t0[:, co, :], A_otr[7][:, :], OP.mult))
            tok_zc = mm_group(co % 4,
                              lambda k, co=co: AW2[:, k, co * 128:(co + 1) * 128],
                              lambda k: A_u2[:, k, :], NFT, ev_z)
        p34["u2_pe"] = ("e_pe", P.cnt["e_pe"])
        P.wait("act", tok_zc)
        p34["zst"] = P.dma("act", dts.ap()[c].rearrange("p (a b) -> p a b", a=NCT),
                           A_t0[:, :, :])
        if DBG:
            P.wait("pool", tok_zc)
            tok_dz = P.dma("pool", dbg["dbg_z"].ap()
                           .rearrange("(ct p) t -> p ct t", p=128)
                           [:, :, c * CH:(c + 1) * CH], A_t0[:, :, :])
            P.wait("dve", tok_dz)
            if c == 0:
                P.wait("pool", tok_u)
                tok_du = P.dma("pool", dbg["dbg_uov"].ap()
                               .rearrange("(h p) t -> p h t", p=128),
                               A_u2[:, :, 0:CH])
                P.wait("act", tok_du)

    P.wait("pe", pT_free[0])
    tok_te = P.do("pe", lambda e: e.transpose(
        pT[0][0:NTT, 0:128], eid_tiles[:, 0:NTT], ident_f[:, :]))
    P.wait("act", tok_te)
    tok_eidr = P.do("act", lambda e: e.copy(A_dump[0:NTT, 0:128],
                                            pT[0][0:NTT, 0:128]))
    pT_free[0] = tok_eidr
    P.wait("sp", tok_eidr)
    tok_eidd = P.dma("sp", deid.ap().rearrange("(a b) -> a b", b=128),
                     A_dump[0:NTT, 0:128])
    if DBG:
        P.wait("pool", tok_eidd)
        P.dma("pool", dbg["dbg_eid"].ap(), deid.ap())

    HW = T // 2
    NQ = (HW + 511) // 512
    t0row = A_t0[:, :, :].rearrange("p a b -> p (a b)")
    ohb = A_xtm[0]
    cumb = A_xtm[1]
    onesEb = cst[0:E, CVALS.index(1.0):CVALS.index(1.0) + 1]

    def oh_half(h, tok_eid_row):
        P.wait_all("pe", [tok_eid_row] + [bank_free[q % 4] for q in range(NQ)])
        mm = []
        for q in range(NQ):
            qw = min(512, HW - q * 512)
            mm.append(P.do("pe", lambda e, q=q, qw=qw: e.matmul(
                pM[q % 4][0:E, 0:qw], ones_row[0:1, 0:E],
                t0row[0:1, q * 512:q * 512 + qw], start=True, stop=True)))
        tok_oh = None
        for q in range(NQ):
            qw = min(512, HW - q * 512)
            P.wait("dve", mm[q])
            tok_oh = P.do("dve", lambda e, q=q, qw=qw: e.tensor_scalar(
                ohb[0:E, q * 512:q * 512 + qw], pM[q % 4][0:E, 0:qw],
                iota8[0:E, 0:1], None, OP.is_equal))
            bank_free[q % 4] = tok_oh
        return tok_oh

    P.wait("sp", tok_eidd)
    tok_rloc_st = []
    tok_eidr2 = None
    for h in range(2):
        P.wait("sp", tok_eidr2)
        tok_er = P.dma("sp", t0row[0:1, 0:HW],
                       deid.ap().rearrange("(a b) -> a b", a=1)
                       [0:1, h * HW:(h + 1) * HW])
        tok_oh = oh_half(h, tok_er)
        tok_eidr2 = ("e_pe", P.cnt["e_pe"])
        tok_cum = P.do("dve", lambda e, h=h: e.tensor_tensor_scan(
            cumb[0:E, 0:HW], onesEb.broadcast_to([E, HW]), ohb[0:E, 0:HW],
            0.0 if h == 0 else carry8[0:E, 0:1], OP.mult, OP.add))
        P.do("dve", lambda e: e.tensor_copy(
            carry8[0:E, 0:1], cumb[0:E, HW - 1:HW]))
        tok_m = P.do("dve", lambda e: e.tensor_mul(
            cumb[0:E, 0:HW], cumb[0:E, 0:HW], ohb[0:E, 0:HW]))
        P.wait_all("pe", [tok_m, pB_free[1]])
        P.wait_all("dve", tok_rloc_st)
        tok_rl = None
        for q in range(NQ):
            qw = min(512, HW - q * 512)
            tok_cs = P.do("pe", lambda e, q=q, qw=qw: e.matmul(
                pB[1][0:1, 0:qw], ones_col[0:E, 0:1],
                cumb[0:E, q * 512:q * 512 + qw], start=True, stop=True))
            P.wait("dve", tok_cs)
            tok_rl = P.do("dve", lambda e, q=q, qw=qw: e.tensor_copy(
                A_dump[0:1, q * 512:q * 512 + qw], pB[1][0:1, 0:qw]))
            P.wait("pe", tok_rl)
        pB_free[1] = tok_rl
        P.wait("sp", tok_rl)
        tok_rloc_st.append(P.dma("sp", dstat.ap()[0:1, h * HW:(h + 1) * HW],
                                 A_dump[0:1, 0:HW]))
    tok_cntd = P.dma("sp", dcnt.ap().rearrange("(a b) -> a b", b=1), carry8[0:E, 0:1])
    P.wait_all("pool", [tok_cntd])
    P.do("pool", lambda e: e.collective_compute(
        "AllGather", OP.bypass, replica_groups=[list(range(n_cores))],
        ins=[dcnt.ap().opt()], outs=[dcall.ap().opt()]), inc="c_cnt")

    P.wait("pool", ("c_cnt", 1))
    tok_call = P.dma("pool", call_sb[0:n_cores, 0:E],
                     dcall.ap().rearrange("(r e) -> r e", e=E))
    if DBG:
        P.dma("pool", dbg["dbg_cnt"].ap(), dcnt.ap())
        P.dma("pool", dbg["dbg_call"].ap(), dcall.ap())
    P.wait_all("pe", [tok_call, tok_pfx] + pB_free)
    if DBG:
        P.wait_all("pool", [tok_call, tok_pfx])
        P.dma("pool", dbg["dbg_pfx"].ap(), pfx_sb[0:8, 0:1])
        P.dma("pool", dbg["dbg_csb"].ap(), call_sb[0:8, 0:E])
    tok_pb = P.do("dve", lambda e: e.tensor_scalar_mul(
        A_otr[7][0:8, 0:CH],
        cst[0:8, CVALS.index(1.0):CVALS.index(1.0) + 1].broadcast_to([8, CH]),
        pfx_sb[0:8, 0:1]))
    P.wait("pe", tok_pb)
    tok_bmm = P.do("pe", lambda e: e.matmul(
        pB[0][0:E, 0:CH], call_sb[0:8, 0:E], A_otr[7][0:8, 0:CH],
        start=True, stop=True))
    P.wait("dve", tok_bmm)
    if DBG:
        tok_dd = P.do("dve", lambda e: e.tensor_copy(A_otr[6][0:8, 0:16],
                                                     pB[0][0:E, 0:16]))
        P.wait("pool", tok_dd)
        P.dma("pool", dbg["dbg_pbc"].ap(), A_otr[7][0:8, 0:16])
        tok_dd2 = P.dma("pool", dbg["dbg_brow"].ap(), A_otr[6][0:8, 0:16])
        P.wait("dve", tok_dd2)
    P.do("dve", lambda e: e.tensor_copy(base8[0:E, 0:2], pB[0][0:E, 0:2]))
    tok_b8 = P.do("dve", lambda e: e.tensor_scalar_add(base8[0:E, 1:2],
                                                       base8[0:E, 0:1], -1.0))
    pB_free[0] = tok_b8

    QW = min(512, HW)
    ddf_all = []
    wq_tok = list(tok_rloc_st)
    f_pass1 = ("e_pe", P.cnt["e_pe"])
    prev_dst = None
    for h in range(2):
        P.wait_all("sp", [tok_eidr2, f_pass1, prev_dst])
        tok_er = P.dma("sp", t0row[0:1, 0:HW],
                       deid.ap().rearrange("(a b) -> a b", a=1)
                       [0:1, h * HW:(h + 1) * HW])
        tok_rll = P.dma("sp", cumb[0:1, 0:HW],
                        dstat.ap()[0:1, h * HW:(h + 1) * HW])
        tok_oh = oh_half(h, tok_er)
        tok_eidr2 = ("e_pe", P.cnt["e_pe"])
        P.wait_all("pe", [tok_oh, tok_b8, pB_free[1]])
        for q in range(NQ):
            qw = min(512, HW - q * 512)
            P.wait("pe", pB_free[1])
            tok_bs = P.do("pe", lambda e, q=q, qw=qw: e.matmul(
                pB[1][0:1, 0:qw], base8[0:E, 1:2],
                ohb[0:E, q * 512:q * 512 + qw], start=True, stop=True))
            P.wait_all("dve", [tok_bs, tok_rll] + wq_tok)
            wq_tok = []
            P.do("dve", lambda e, q=q, qw=qw: e.tensor_tensor(
                A_dump[0:1, 0:qw], cumb[0:1, q * 512:q * 512 + qw],
                pB[1][0:1, 0:qw], OP.add))
            P.do("dve", lambda e, qw=qw: e.tensor_scalar(
                A_dump[0:1, 2 * QW:2 * QW + qw], A_dump[0:1, 0:qw], float(CAP),
                -1.0, OP.is_lt, OP.mult))
            tok_ivm = P.do("dve", lambda e, qw=qw: e.tensor_scalar_add(
                A_dump[0:1, 2 * QW:2 * QW + qw], A_dump[0:1, 2 * QW:2 * QW + qw], 1.0))
            P.do("dve", lambda e, q=q, qw=qw: e.scalar_tensor_tensor(
                A_dump[0:1, QW:QW + qw], t0row[0:1, q * 512:q * 512 + qw],
                float(CAP), A_dump[0:1, 0:qw], OP.mult, OP.add))
            tok_dst = P.do("dve", lambda e, qw=qw: e.scalar_tensor_tensor(
                A_dump[0:1, QW:QW + qw], A_dump[0:1, 2 * QW:2 * QW + qw], BIG,
                A_dump[0:1, QW:QW + qw], OP.mult, OP.add))
            pB_free[1] = tok_dst
            prev_dst = tok_dst
            P.wait("sp", tok_ivm)
            P.wait("sp", tok_dst)
            t1_ = P.dma("sp", divm.ap().rearrange("(a b) -> a b", a=1)
                        [0:1, h * HW + q * 512:h * HW + q * 512 + qw],
                        A_dump[0:1, 2 * QW:2 * QW + qw])
            t2__ = P.dma("sp", ddf.ap().rearrange("(a b) -> a b", a=1)
                         [0:1, h * HW + q * 512:h * HW + q * 512 + qw],
                         A_dump[0:1, QW:QW + qw])
            ddf_all.extend([t1_, t2__])
            wq_tok = [t1_, t2__]
            if DBG:
                P.wait("pool", tok_dst)
                tok_d = P.dma("pool", dbg["dbg_rank"].ap()
                              .rearrange("(a b) -> a b", a=1)
                              [0:1, h * HW + q * 512:h * HW + q * 512 + qw],
                              A_dump[0:1, 0:qw])
                P.wait("dve", tok_d)
                wq_tok.append(tok_d)
                ddf_all.append(tok_d)

    P.wait_all("sp", ddf_all)
    tok_dl = P.dma("sp", A_dump[0:NTT, 0:128],
                   ddf.ap().rearrange("(a b) -> a b", b=128))
    P.wait_all("pe", [tok_dl, pT_free[0]])
    tok_dtr = P.do("pe", lambda e: e.transpose(
        pT[0][0:128, 0:NTT], A_dump[0:NTT, 0:128], ident_f[0:NTT, 0:NTT]))
    P.wait("dve", tok_dtr)
    tok_di = P.do("dve", lambda e: e.tensor_copy(desti_t[:, 0:NTT],
                                                 pT[0][0:128, 0:NTT]))
    pT_free[0] = tok_di
    if DBG:
        P.wait("pool", tok_di)
        P.dma("pool", dbg["dbg_dest"].ap(), ddf.ap())
        P.dma("pool", dbg["dbg_ivm"].ap(), divm.ap())

    emit_zero_copies(n_zdma)
    P.wait_all("pool", [tok_di] + zfill_done)
    P.wait_all("pool", P.dma_fence("pool"))
    P.wait_all("pool", ddf_all)
    if NCT * CH >= 2 * D:
        scat_buf = [xlb_flat[:, 0:D], xlb_flat[:, D:2 * D]]
    else:
        scat_buf = [xlb_flat[:, 0:D], xlb_flat[:, 0:D]]
    scat_tok = [None, None]
    for g in range(0 if cfg.get("skip_scat") else NTT):
        sbuf_g = scat_buf[g % 2]
        P.wait_all("pool", [scat_tok[g % 2]])
        tok_ld = P.dma("pool", sbuf_g,
                       dt2tm.ap()[g * 128:(g + 1) * 128, :])
        P.wait("pool", tok_ld)
        scat_tok[g % 2] = P.idma("pool", lambda e, g=g, sbuf_g=sbuf_g:
                                 e.indirect_dma_start(
                                     dslot.ap(),
                                     IndirectOffsetOnAxis(desti_t[:, g:g + 1], 0),
                                     sbuf_g, None,
                                     bounds_check=rbnds[0],
                                     oob_is_err=False))
    P.wait_all("pool", P.dma_fence("pool"))
    P.do("pool", lambda e: e.collective_compute(
        "ReduceScatter", OP.add, replica_groups=[list(range(n_cores))],
        ins=[dslot.ap().opt()], outs=[dmybuf.ap().opt()]), inc="c_rs")

    for c in range(NCH):
        yov_chunk(c)

    P.wait("act", p34.get("rwb_pe"))
    P.wait_all("act", tok_cv_w)
    P.wait_all("sp", tok_cv_w)
    tok_w1 = P.dma("act", AW1[:, :, 0:DF],
                   dw1b.ap().rearrange("(c p) f -> p c f", p=128))
    P.wait("sp", p34.get("u2_pe"))
    tok_w2 = P.dma("sp", AW2[:, :, :],
                   dw2b.ap().rearrange("(h p) f -> p h f", p=128))
    P.wait("pool", ("c_rs", 1))
    if DBG:
        P.dma("pool", dbg["dbg_buf"].ap(), dmybuf.ap())
    P.wait_all("pe", [tok_w1, tok_w2])

    RCB = TTC * 128
    nrcb = (CAP + RCB - 1) // RCB
    tok_yst = None
    for rb in range(nrcb):
        r0 = rb * RCB
        nr = min(RCB, CAP - r0)
        ntile = (nr + 127) // 128
        P.wait_all("pool", P.fence("pe"))
        P.wait_all("pool", P.fence("act"))
        P.wait("act", tok_yst)
        tok_bl = P.dma("pool", A_xtm[0][:, 0:ntile * D]
                       .rearrange("p (tt f) -> p tt f", tt=ntile),
                       dmybuf.ap()[r0:r0 + nr, :]
                       .rearrange("(tt p) f -> p tt f", p=128))
        P.wait("pe", tok_bl)
        for ct in range(NCT):
            bi = ct % 2
            P.wait("pe", pT_free[bi])
            tok_tr = None
            for tt2 in range(ntile):
                tok_tr = P.do("pe", lambda e, ct=ct, tt2=tt2, bi=bi: e.transpose(
                    pT[bi][0:128, tt2 * 128:(tt2 + 1) * 128],
                    A_xtm[0][:, tt2 * D + ct * 128: tt2 * D + (ct + 1) * 128],
                    ident_f[:, :]))
            P.wait("dve", tok_tr)
            tok_bcm = P.do("dve", lambda e, ct=ct, bi=bi, nr=nr: e.tensor_copy(
                A_rwb[:, ct, 0:nr], pT[bi][0:128, 0:nr]))
            pT_free[bi] = tok_bcm
            P.wait("pe", tok_bcm)
        tok_h = None
        for h in range(NFT):
            def ev_h(bk, tok_mm, h=h, nr=nr):
                P.wait("act", tok_mm)
                return P.do("act", lambda e, bk=bk, h=h, nr=nr: e.activation(
                    A_u2[:, h, 0:nr], bk[:, 0:nr], AF.Relu,
                    bias=cb1e[:, h:h + 1]), nochain=(h > 0))
            tok_h = mm_group(h % 4,
                             lambda k, h=h: AW1[:, k, h * 128:(h + 1) * 128],
                             lambda k, nr=nr: A_rwb[:, k, 0:nr], NCT, ev_h, N=nr)
        P.wait("pe", tok_h)
        tok_yev = None
        for co in range(NCT):
            def ev_y(bk, tok_mm, co=co, nr=nr):
                P.wait("dve", tok_mm)
                return P.do("dve", lambda e, bk=bk, co=co, nr=nr: e.tensor_scalar_add(
                    A_otr[co % 2][:, 0:nr], bk[:, 0:nr], cb2e[:, co:co + 1]),
                    nochain=(co > 1))
            tok_y = mm_group(co % 4,
                             lambda k, co=co: AW2[:, k, co * 128:(co + 1) * 128],
                             lambda k, nr=nr: A_u2[:, k, 0:nr], NFT, ev_y, N=nr)
            P.wait("pe", tok_y)
            bi = co % 2
            P.wait("pe", pT_free[bi])
            tok_ytr = None
            for tt2 in range(ntile):
                tok_ytr = P.do("pe", lambda e, co=co, tt2=tt2, bi=bi: e.transpose(
                    pT[bi][0:128, tt2 * 128:(tt2 + 1) * 128],
                    A_otr[co % 2][:, tt2 * 128:(tt2 + 1) * 128], ident_f[:, :]))
            P.wait("act", tok_ytr)
            for tt2 in range(ntile):
                tok_yev = P.do("act", lambda e, co=co, tt2=tt2, bi=bi: e.copy(
                    A_xtm[1][:, tt2 * D + co * 128: tt2 * D + (co + 1) * 128],
                    pT[bi][0:128, tt2 * 128:(tt2 + 1) * 128]))
            pT_free[bi] = tok_yev
        P.wait("sp", tok_yev)
        tok_yst = P.dma("sp", dytm.ap()[r0:r0 + nr, :]
                        .rearrange("(tt p) f -> p tt f", p=128),
                        A_xtm[1][:, 0:ntile * D]
                        .rearrange("p (tt f) -> p tt f", tt=ntile))
    if DBG:
        P.wait_all("pool", P.dma_fence("sp"))
        P.dma("pool", dbg["dbg_y"].ap(), dytm.ap())

    P.wait_all("pool", P.dma_fence("sp"))
    P.do("pool", lambda e: e.collective_compute(
        "AllGather", OP.bypass, replica_groups=[list(range(n_cores))],
        ins=[dytm.ap().opt()], outs=[dyall.ap().opt()]), inc="c_yag")
    P.wait("pool", ("c_yag", 1))
    tok_yag_done = ("c_yag", P.cnt["c_yag"])

    chunk_f[0] = P.fence_all()
    chunk_f[1] = P.fence_all()

    p4s = {}

    def p4_chunk(c):
        pr = c % 2
        xt = A_xtm[pr]
        if c == 0:
            P.wait_all("sp", P.dma_fence("sp"))
            P.wait_all("sp", P.dma_fence("act"))
            P.wait("act", tok_yst)
        P.wait("sp", p4s.get("t0_free"))
        tok_zl = P.dma("sp", A_t0[:, :, :],
                       dts.ap()[c].rearrange("p (a b) -> p a b", a=NCT))
        xo = A_xtm[(c + 1) % 2]
        P.wait("act", p4s.get("out_dma"))
        P.wait("act", p4s.get("fin_dve"))
        tok_zm = cm_to_tm(A_t0, xo, tok_zl)
        p4s["t0_free"] = ("e_pe", P.cnt["e_pe"])

        P.wait("act", p4s.get("fin_dve"))
        tok_X2 = P.dma("act", xt.rearrange("p (tt f) -> p tt f", tt=TTC),
                       dX2.ap()[c * CH:(c + 1) * CH, :]
                       .rearrange("(tt p) f -> p tt f", p=128))
        P.wait("pool", tok_yag_done)
        P.wait("pool", p4s.get("fin_dve"))
        P.do("pool", lambda e: e.memset(xlf_flat[:, 0:TTC * D], 0.0))
        toks_g = []
        gv = cfg.get("gvariant", 0)
        for tt in range(TTC):
            g = c * TTC + tt
            if gv == 1:
                toks_g.append(P.idma("pool", lambda e, g=g, tt=tt: e.indirect_dma_start(
                    xlf_flat[:, tt * D:(tt + 1) * D], None,
                    dyall.ap(), IndirectOffsetOnAxis(desti_t[:, 0:1], 0),
                    bounds_check=rbnds[0], oob_is_err=False)))
            elif gv == 2:
                toks_g.append(P.idma("pool", lambda e, g=g, tt=tt: e.indirect_dma_start(
                    xlb_flat[:, 0:D], None,
                    dslot.ap(), IndirectOffsetOnAxis(desti_t[:, g:g + 1], 0),
                    bounds_check=rbnds[0], oob_is_err=False)))
            else:
                toks_g.append(P.idma("pool", lambda e, g=g, tt=tt: e.indirect_dma_start(
                    xlf_flat[:, tt * D:(tt + 1) * D], None,
                    dyall.ap(), IndirectOffsetOnAxis(desti_t[:, g:g + 1], 0),
                    bounds_check=rbnds[0], oob_is_err=False)))
        P.wait_all("dve", [tok_X2, tok_zm] + toks_g)
        tok_fin = None
        for tt in range(TTC):
            P.do("dve", lambda e, tt=tt, xo=xo: e.tensor_tensor(
                xt[:, tt * D:(tt + 1) * D], xt[:, tt * D:(tt + 1) * D],
                xo[:, tt * D:(tt + 1) * D], OP.add), nochain=(tt > 0))
            tok_fin = P.do("dve", lambda e, tt=tt: e.tensor_tensor(
                xt[:, tt * D:(tt + 1) * D], xt[:, tt * D:(tt + 1) * D],
                xlf_flat[:, tt * D:(tt + 1) * D], OP.add))
        p4s["fin_dve"] = tok_fin
        P.wait("sp", tok_fin)
        p4s["out_dma"] = P.dma("sp", dout.ap()[c * CH:(c + 1) * CH, :]
                               .rearrange("(tt p) f -> p tt f", p=128),
                               xt.rearrange("p (tt f) -> p tt f", tt=TTC))

    for c in range(NCH):
        p4_chunk(c)

    P.wait_all("pool", P.fence_all())

    with stack:
        with ExitStack() as semstack:
            for s in list(P.cnt.keys()):
                if s.startswith("n_"):
                    continue
                P.sems[s] = semstack.enter_context(nc.semaphore(s))
            with nc.Block() as block:
                @block.tensor
                def _(e):
                    for f in P.streams["pe"]:
                        f(e)

                @block.vector
                def _(e):
                    for f in P.streams["dve"]:
                        f(e)

                @block.scalar
                def _(e):
                    for f in P.streams["act"]:
                        f(e)

                @block.gpsimd
                def _(e):
                    for f in P.streams["pool"]:
                        f(e)

                @block.sync
                def _(e):
                    for f in P.streams["sp"]:
                        f(e)
    return nc



def _prep_inputs(inputs, cfg):
    E = cfg["E"]
    n = cfg["n_cores"]
    x = np.asarray(inputs["L0_attn_x"])
    tr = lambda a: np.ascontiguousarray(np.asarray(a).T)
    cc = lambda a: np.ascontiguousarray(np.asarray(a))
    shared = dict(
        WkT=tr(inputs["L0_attn_Wk"]), WvT=tr(inputs["L0_attn_Wv"]),
        WrT=tr(inputs["L0_attn_Wr"]), WoT=tr(inputs["L0_attn_Wo"]),
        WffkT=tr(inputs["L0_attn_Wffk"]), WffvT=tr(inputs["L0_attn_Wffv"]),
        td=cc(inputs["L0_attn_time_decay"]).reshape(-1),
        tf=cc(inputs["L0_attn_time_first"]).reshape(-1),
        ln1g=cc(inputs["L0_attn_ln1_g"]), ln1b=cc(inputs["L0_attn_ln1_b"]),
        ln2g=cc(inputs["L0_attn_ln2_g"]), ln2b=cc(inputs["L0_attn_ln2_b"]),
        router_w=cc(inputs["L0_ffn_router_w"]),
        router_b=cc(inputs["L0_ffn_router_b"]),
        ow1=cc(inputs["L0_ffn_ow1"]), ob1=cc(inputs["L0_ffn_ob1"]),
        ow2=cc(inputs["L0_ffn_ow2"]), ob2=cc(inputs["L0_ffn_ob2"]),
    )
    in_maps = []
    for i in range(n):
        m = dict(shared)
        m["x"] = cc(x[i])
        e = i % E
        m["w1e"] = cc(inputs["L0_ffn_w1"][e])
        m["b1e"] = cc(inputs["L0_ffn_b1"][e])
        m["w2e"] = cc(inputs["L0_ffn_w2"][e])
        m["b2e"] = cc(inputs["L0_ffn_b2"][e])
        m["prefix_mask"] = (np.arange(8) < i).astype(np.float32)
        in_maps.append(m)
    return in_maps


def kernel(**inputs):
    cfg = make_cfg()
    nc = build_nc(cfg)
    in_maps = _prep_inputs(inputs, cfg)
    res = run_bass_kernel_spmd(nc, in_maps, core_ids=list(range(cfg["n_cores"])))
    out = np.stack([res.results[i]["out"] for i in range(cfg["n_cores"])], axis=0)
    return out.astype(np.float32)
